# version 01
# baseline (speedup 1.0000x reference)
import concourse.bass as bass
import concourse.mybir as mybir

F32 = mybir.dt.float32
BF16 = mybir.dt.bfloat16
AF = mybir.ActivationFunctionType
ALU = mybir.AluOpType
AX = mybir.AxisListType.X


def build_k1(nc: bass.Bass, E):
    xb = nc.dram_tensor("xroll", [4096, 2048], F32, kind="ExternalInput").ap()
    winT = nc.dram_tensor("winT", [2048, 2048], F32, kind="ExternalInput").ap()
    wclsT = nc.dram_tensor("wclsT", [2048, 64], F32, kind="ExternalInput").ap()
    biasb = nc.dram_tensor("biasb", [128, 64], F32, kind="ExternalInput").ap()
    apar = nc.dram_tensor("apar", [128, 1], F32, kind="ExternalInput").ap()
    ident = nc.dram_tensor("ident", [128, 128], BF16, kind="ExternalInput").ap()

    h_out = nc.dram_tensor("h_out", [1024, 4096], BF16, kind="ExternalOutput").ap()
    wr_out = nc.dram_tensor("wr_out", [4096, 32], F32, kind="ExternalOutput").ap()
    f_out = nc.dram_tensor("f_out", [4096, 1], F32, kind="ExternalOutput").ap()

    a_dram = nc.dram_tensor("a_spill", [1024, 4096], F32, kind="Internal").ap()
    w_dram = nc.dram_tensor("w_spill", [1024, 4096], BF16, kind="Internal").ap()
    z_dram = nc.dram_tensor("z_spill", [1024, 4096], BF16, kind="Internal").ap()

    xn_fm = E(nc.sbuf_tensor("xn_fm", [128, 16, 2048], BF16))
    s_id = E(nc.sbuf_tensor("s_id", [128, 128], BF16))
    s_bias = E(nc.sbuf_tensor("s_bias", [128, 64], F32))
    s_apar = E(nc.sbuf_tensor("s_apar", [128, 1], F32))
    m8af = E(nc.sbuf_tensor("m8af", [128, 1], F32))
    sp_t = E(nc.sbuf_tensor("sp_t", [128, 2], F32))
    stc = [E(nc.sbuf_tensor(f"stc{k}", [128, 8], F32)) for k in range(2)]
    wc32 = E(nc.sbuf_tensor("wc32", [128, 16, 64], F32))
    wcb = E(nc.sbuf_tensor("wcb", [128, 16, 64], BF16))
    e_sb = [E(nc.sbuf_tensor(f"e_sb{k}", [128, 64], F32)) for k in range(2)]
    rc = [E(nc.sbuf_tensor(f"rc{k}", [128, 8], F32)) for k in range(2)]
    r_sc = E(nc.sbuf_tensor("r_sc", [128, 64], F32))
    r_sp = E(nc.sbuf_tensor("r_sp", [128, 64], F32))
    r_sel = E(nc.sbuf_tensor("r_sel", [128, 64], F32))
    r_m8 = E(nc.sbuf_tensor("r_m8", [128, 8], F32))
    wr_sb = [E(nc.sbuf_tensor(f"wr_sb{k}", [128, 64], F32)) for k in range(2)]
    f_sb = [E(nc.sbuf_tensor(f"f_sb{k}", [128, 1], F32)) for k in range(2)]
    arena = E(nc.sbuf_tensor("arena", [128, 28672], F32))
    ar32 = arena
    ar16 = arena.bitcast(BF16)
    def cv32(byte_off, cols, shape=None):
        ap = ar32[:, byte_off // 4: byte_off // 4 + cols]
        return ap if shape is None else ap.rearrange(
            "p (" + " ".join(chr(97 + i) for i in range(len(shape))) + ") -> p "
            + " ".join(chr(97 + i) for i in range(len(shape))),
            **{chr(97 + i): s for i, s in enumerate(shape)})
    def cv16(byte_off, cols, shape=None):
        ap = ar16[:, byte_off // 2: byte_off // 2 + cols]
        return ap if shape is None else ap.rearrange(
            "p (" + " ".join(chr(97 + i) for i in range(len(shape))) + ") -> p "
            + " ".join(chr(97 + i) for i in range(len(shape))),
            **{chr(97 + i): s for i, s in enumerate(shape)})
    xt = [cv32(0, 2048), cv32(8192, 2048)]
    sq = cv32(16384, 2048)
    xn_tm = [cv16(24576, 2048), cv16(28672, 2048)]
    w32 = [cv32(32768, 4096, (2, 16, 128)), cv32(49152, 4096, (2, 16, 128))]
    wb = [cv16(65536, 4096, (2, 16, 128)), cv16(73728, 4096, (2, 16, 128))]
    gxs = [cv32(81920, 512), cv32(83968, 512)]
    gas = [cv32(86016, 512), cv32(88064, 512)]
    t1 = cv32(90112, 512)
    gs2 = [cv32(92160, 512), cv32(94208, 512)]
    a_sb = [cv32(96256, 512), cv32(98304, 512)]
    u_sb = [cv32(100352, 512), cv32(102400, 512)]
    w_sb = [cv16(104448, 512), cv16(105472, 512)]
    z_sb = [cv16(106496, 512), cv16(107520, 512)]
    a_ld = [cv32(0, 4096), cv32(16384, 4096)]
    z_ld = [cv16(32768, 4096), cv16(40960, 4096)]
    wl_ld = [cv16(49152, 4096), cv16(57344, 4096)]
    mu_sb = [cv16(65536, 4096), cv16(73728, 4096)]
    b_sb = cv32(81920, 4096)
    hb_sb = [cv16(98304, 4096), cv16(106496, 4096)]

    tr = [E(nc.psum_tensor(f"tr{k}", [128, 1024], BF16)) for k in range(2)]
    pgx = [E(nc.psum_tensor(f"pgx{k}", [128, 512], F32)) for k in range(2)]
    pga = [E(nc.psum_tensor(f"pga{k}", [128, 512], F32)) for k in range(2)]
    pr = [E(nc.psum_tensor(f"pr{k}", [128, 64], F32)) for k in range(2)]

    sem_names = ["sLC", "sLX0", "sLX1", "sSQ", "sVS", "sRS", "sNB", "sXN", "sTP",
                 "sTE", "sLW0", "sLW1", "sWC", "sPE", "sTH", "sGS",
                 "sA", "sW", "sZ", "sSPA0", "sSPA1", "sSPW0", "sSPW1",
                 "sSPZ0", "sSPZ1", "sLR", "sWCR",
                 "sRT", "sRM", "sRE", "sWR", "sSWR0", "sSWR1", "sM8",
                 "sL20", "sL21", "sMU", "sHB", "sSH0", "sSH1", "sSC", "sSS", "sSV"]
    S = {n: E(nc.semaphore(n)) for n in sem_names}
    block = E(nc.Block())

    @block.sync
    def _(sync):
        sync.dma_start(out=s_id[:], in_=ident).then_inc(S["sLC"], 16)
        sync.dma_start(out=s_bias[:], in_=biasb).then_inc(S["sLC"], 16)
        sync.dma_start(out=s_apar[:], in_=apar).then_inc(S["sLC"], 16)
        for blk in range(2):
            for t in range(16):
                g = blk * 16 + t
                if g >= 2:
                    sync.wait_ge(S["sXN"], g - 1)
                sync.dma_start(out=xt[g % 2][:],
                               in_=xb[128 * g:128 * (g + 1), :]).then_inc(S[f"sLX{g % 2}"], 16)
            for i in range(8):
                P = blk * 8 + i
                if P >= 2:
                    sync.wait_ge(S["sWC"], P - 1)
                sync.dma_start(
                    out=w32[P % 2][:, 0],
                    in_=winT[:, 128 * i:128 * (i + 1)]
                    .rearrange("(hc p) g -> p hc g", p=128)).then_inc(S[f"sLW{P % 2}"], 16)
                sync.dma_start(
                    out=w32[P % 2][:, 1],
                    in_=winT[:, 1024 + 128 * i:1024 + 128 * (i + 1)]
                    .rearrange("(hc p) g -> p hc g", p=128)).then_inc(S[f"sLW{P % 2}"], 16)
            if blk == 0:
                sync.dma_start(
                    out=wc32[:],
                    in_=wclsT.rearrange("(hc p) g -> p hc g", p=128)
                ).then_inc(S["sLR"], 16)
        for q in range(2):
            sync.wait_ge(S[f"sSPA{q}"], 16 * 32)
            sync.wait_ge(S[f"sSPW{q}"], 16 * 32)
            sync.wait_ge(S[f"sSPZ{q}"], 16 * 32)
        for i in range(8):
            if i >= 2:
                sync.wait_ge(S["sSC"], i - 1)
            sync.dma_start(out=a_ld[i % 2][:],
                           in_=a_dram[128 * i:128 * (i + 1), :]).then_inc(S[f"sL2{i % 2}"], 16)
            sync.dma_start(out=z_ld[i % 2][:],
                           in_=z_dram[128 * i:128 * (i + 1), :]).then_inc(S[f"sL2{i % 2}"], 16)
            sync.dma_start(out=wl_ld[i % 2][:],
                           in_=w_dram[128 * i:128 * (i + 1), :]).then_inc(S[f"sL2{i % 2}"], 16)

    @block.scalar
    def _(scalar):
        cS = [0]
        def sinc(ins):
            cS[0] += 1
            ins.then_inc(S["sSS"], 1)
        def swait():
            scalar.wait_ge(S["sSS"], cS[0])
        scalar.wait_ge(S["sLC"], 48)
        sinc(scalar.activation(sp_t[:, 0:1], s_apar[:], AF.Exp))
        swait()
        sinc(scalar.add(sp_t[:, 1:2], sp_t[:, 0:1], 1.0))
        swait()
        sinc(scalar.activation(sp_t[:, 0:1], sp_t[:, 1:2], AF.Ln))
        swait()
        scalar.mul(m8af[:], sp_t[:, 0:1], -8.0).then_inc(S["sM8"], 1)
        for blk in range(2):
            for t in range(16):
                g = blk * 16 + t
                p = g % 2
                scalar.wait_ge(S[f"sLX{g % 2}"], 16 * (g // 2 + 1))
                if g >= 2:
                    scalar.wait_ge(S["sVS"], g - 1)
                scalar.activation(sq[:], xt[p][:], AF.Square,
                                  accum_out=stc[p][:, 0:1]).then_inc(S["sSQ"], 1)
                scalar.wait_ge(S["sVS"], g + 1)
                sinc(scalar.activation(stc[p][:, 4:5], stc[p][:, 3:4], AF.Ln))
                swait()
                scalar.activation(stc[p][:, 5:6], stc[p][:, 4:5], AF.Exp,
                                  scale=-0.5).then_inc(S["sRS"], 1)
                scalar.wait_ge(S["sNB"], g + 1)
                if g >= 2:
                    scalar.wait_ge(S["sTP"], 2 * g - 2)
                scalar.activation(xn_tm[p][:], xt[p][:], AF.Identity,
                                  bias=stc[p][:, 6:7], scale=stc[p][:, 5:6]
                                  ).then_inc(S["sXN"], 1)
            for i in range(8):
                P = blk * 8 + i
                for tc in range(4):
                    u = 4 * P + tc
                    scalar.wait_ge(S["sPE"], u + 1)
                    if u >= 2:
                        scalar.wait_ge(S["sGS"], u - 1)
                    scalar.activation(gxs[u % 2][:], pgx[u % 2][:], AF.Tanh,
                                      scale=0.5)
                    scalar.activation(gas[u % 2][:], pga[u % 2][:], AF.Tanh,
                                      scale=0.5).then_inc(S["sTH"], 1)
                    scalar.wait_ge(S["sGS"], u + 1)
                    scalar.wait_ge(S["sM8"], 1)
                    if u >= 2:
                        scalar.wait_ge(S[f"sSPA{u % 2}"], 16 * (u // 2))
                    if u >= 2:
                        scalar.wait_ge(S["sZ"], u - 1)
                    sinc(scalar.activation(a_sb[u % 2][:], gs2[u % 2][:], AF.Exp,
                                      scale=m8af[:, 0:1]))
                    swait()
                    scalar.activation(u_sb[u % 2][:], a_sb[u % 2][:], AF.Square
                                      ).then_inc(S["sA"], 1)
            for t in range(16):
                rt = blk * 16 + t
                scalar.wait_ge(S["sRM"], rt + 1)
                if rt >= 2:
                    scalar.wait_ge(S["sWR"], rt - 1)
                scalar.activation(e_sb[rt % 2][:], pr[rt % 2][:], AF.Exp,
                                  bias=rc[rt % 2][:, 0:1]).then_inc(S["sRE"], 1)
        for i in range(8):
            scalar.wait_ge(S[f"sL2{i % 2}"], 48 * (i // 2 + 1))
            if i >= 2:
                scalar.wait_ge(S["sHB"], i - 1)
            scalar.activation(mu_sb[i % 2][:], z_ld[i % 2][:], AF.Sqrt
                              ).then_inc(S["sMU"], 1)

    @block.vector
    def _(vector):
        cV = [0]
        def vinc(ins):
            cV[0] += 1
            ins.then_inc(S["sSV"], 1)
        def vwait():
            vector.wait_ge(S["sSV"], cV[0])
        for blk in range(2):
            for t in range(16):
                g = blk * 16 + t
                p = g % 2
                vector.wait_ge(S[f"sLX{g % 2}"], 16 * (g // 2 + 1))
                if g >= 2:
                    vector.wait_ge(S["sXN"], g - 1)
                vinc(vector.reduce_sum(stc[p][:, 1:2], xt[p][:], axis=AX, negate=True))
                vwait()
                vinc(vector.tensor_scalar_mul(stc[p][:, 1:2], stc[p][:, 1:2], 1.0 / 2048.0))
                vector.wait_ge(S["sSQ"], g + 1)
                vinc(vector.tensor_scalar_mul(stc[p][:, 2:3], stc[p][:, 0:1], 1.0 / 2048.0))
                vwait()
                vinc(vector.tensor_mul(stc[p][:, 3:4], stc[p][:, 1:2], stc[p][:, 1:2]))
                vwait()
                vinc(vector.tensor_sub(stc[p][:, 3:4], stc[p][:, 2:3], stc[p][:, 3:4]))
                vwait()
                vector.tensor_scalar_add(stc[p][:, 3:4], stc[p][:, 3:4], 1e-5
                                         ).then_inc(S["sVS"], 1)
                vector.wait_ge(S["sRS"], g + 1)
                vwait()
                vector.tensor_mul(stc[p][:, 6:7], stc[p][:, 1:2], stc[p][:, 5:6]
                                  ).then_inc(S["sNB"], 1)
                for gg in range(2):
                    tg = 2 * g + gg
                    vector.wait_ge(S["sTP"], tg + 1)
                    vector.tensor_copy(
                        xn_fm[:, 8 * gg:8 * (gg + 1), 128 * t:128 * (t + 1)],
                        tr[tg % 2][:].rearrange("p (j t2) -> p j t2", t2=128),
                    ).then_inc(S["sTE"], 1)
            for i in range(8):
                P = blk * 8 + i
                for tc in range(4):
                    u = 4 * P + tc
                    vector.wait_ge(S["sTH"], u + 1)
                    vinc(vector.tensor_scalar(t1[:], gxs[u % 2][:], 0.5, 0.5,
                                         ALU.mult, ALU.add))
                    if u >= 2:
                        vector.wait_ge(S[f"sSPW{u % 2}"], 16 * (u // 2))
                    vwait()
                    vector.tensor_mul(w_sb[u % 2][:], t1[:],
                                      xn_fm[:, i, 512 * tc:512 * (tc + 1)]
                                      ).then_inc(S["sW"], 1)
                    if u >= 2:
                        vector.wait_ge(S["sA"], u - 1)
                    vector.tensor_scalar(gs2[u % 2][:], gas[u % 2][:], 0.5, 0.5,
                                         ALU.mult, ALU.add).then_inc(S["sGS"], 1)
                    vector.wait_ge(S["sA"], u + 1)
                    if u >= 2:
                        vector.wait_ge(S[f"sSPZ{u % 2}"], 16 * (u // 2))
                    vector.tensor_scalar(z_sb[u % 2][:], u_sb[u % 2][:], -1.0, 1.0,
                                         ALU.mult, ALU.add).then_inc(S["sZ"], 1)
            for t in range(16):
                rt = blk * 16 + t
                q = rt % 2
                vector.wait_ge(S["sRT"], rt + 1)
                if rt >= 2:
                    vector.wait_ge(S["sRE"], rt - 1)
                vector.reduce_max(rc[q][:, 0:1], pr[q][:], axis=AX, negate=True
                                  ).then_inc(S["sRM"], 1)
                vector.wait_ge(S["sRE"], rt + 1)
                vinc(vector.reduce_sum(rc[q][:, 1:2], e_sb[q][:], axis=AX))
                vwait()
                vinc(vector.reciprocal(rc[q][:, 2:3], rc[q][:, 1:2]))
                vwait()
                vinc(vector.tensor_scalar_mul(r_sc[:], e_sb[q][:], rc[q][:, 2:3]))
                vwait()
                vinc(vector.tensor_add(r_sp[:], r_sc[:], s_bias[:]))
                vwait()
                vinc(vector.max(out=r_m8[:], in_=r_sp[:]))
                vwait()
                vinc(vector.tensor_single_scalar(r_sel[:], r_sp[:], r_m8[:, 7:8],
                                            ALU.is_ge))
                vwait()
                vinc(vector.tensor_mul(r_sc[:], r_sc[:], r_sel[:]))
                vwait()
                vinc(vector.reduce_sum(rc[q][:, 3:4], r_sc[:], axis=AX))
                vwait()
                vinc(vector.tensor_scalar_add(rc[q][:, 3:4], rc[q][:, 3:4], 1e-20))
                vwait()
                vinc(vector.reciprocal(rc[q][:, 4:5], rc[q][:, 3:4]))
                vwait()
                vinc(vector.tensor_scalar_mul(rc[q][:, 4:5], rc[q][:, 4:5], 1.5))
                if rt >= 2:
                    vector.wait_ge(S[f"sSWR{rt % 2}"], 32 * (rt // 2))
                vwait()
                vinc(vector.tensor_scalar_mul(wr_sb[q][:], r_sc[:], rc[q][:, 4:5]))
                vwait()
                vinc(vector.reduce_sum(f_sb[q][:], wr_sb[q][:, 32:64], axis=AX))
                vwait()
                vinc(vector.tensor_scalar_add(f_sb[q][:], f_sb[q][:], 1.0))
                vwait()
                vector.tensor_scalar_mul(wr_sb[q][:, 0:32], wr_sb[q][:, 0:32],
                                         f_sb[q][:, 0:1]).then_inc(S["sWR"], 1)
        for i in range(8):
            vector.wait_ge(S["sMU"], i + 1)
            if i >= 1:
                vector.wait_ge(S["sHB"], i)
            vinc(vector.tensor_mul(b_sb[:], mu_sb[i % 2][:], wl_ld[i % 2][:]))
            if i >= 2:
                vector.wait_ge(S[f"sSH{i % 2}"], 16 * (i // 2))
            vwait()
            vector.tensor_tensor_scan(hb_sb[i % 2][:], a_ld[i % 2][:], b_sb[:],
                                      0.0, ALU.mult, ALU.add).then_inc(S["sHB"], 1)
            vector.engine_nop().then_inc(S["sSC"], 1)

    @block.tensor
    def _(tensor):
        tensor.wait_ge(S["sLC"], 48)
        for blk in range(2):
            for t in range(16):
                g = blk * 16 + t
                tensor.wait_ge(S["sXN"], g + 1)
                for gg in range(2):
                    tg = 2 * g + gg
                    if tg >= 2:
                        tensor.wait_ge(S["sTE"], tg - 1)
                    for j in range(8):
                        jj = 8 * gg + j
                        ins = tensor.transpose(
                            tr[tg % 2][:, 128 * j:128 * (j + 1)],
                            xn_tm[g % 2][:, 128 * jj:128 * (jj + 1)],
                            s_id[:])
                    ins.then_inc(S["sTP"], 1)
            tensor.wait_ge(S["sTE"], 32 * (blk + 1))
            for i in range(8):
                P = blk * 8 + i
                tensor.wait_ge(S["sWC"], P + 1)
                for tc in range(4):
                    u = 4 * P + tc
                    if u >= 2:
                        tensor.wait_ge(S["sTH"], u - 1)
                    for half in range(2):
                        dst = (pgx if half == 0 else pga)[u % 2]
                        for hc in range(16):
                            ins = tensor.matmul(
                                dst[:],
                                wb[P % 2][:, half, hc, :],
                                xn_fm[:, hc, 512 * tc:512 * (tc + 1)],
                                start=(hc == 0), stop=(hc == 15))
                        if half == 1:
                            ins.then_inc(S["sPE"], 1)
            tensor.wait_ge(S["sWCR"], 1)
            for t in range(16):
                rt = blk * 16 + t
                if rt >= 2:
                    tensor.wait_ge(S["sRE"], rt - 1)
                for hc in range(16):
                    ins = tensor.matmul(
                        pr[rt % 2][:],
                        xn_fm[:, hc, 128 * t:128 * (t + 1)],
                        wcb[:, hc, :],
                        start=(hc == 0), stop=(hc == 15))
                ins.then_inc(S["sRT"], 1)

    @block.gpsimd
    def _(g):
        def spills(P):
            blkP, iP = P // 8, P % 8
            for tc in range(4):
                u = 4 * P + tc
                cs = slice(2048 * blkP + 512 * tc, 2048 * blkP + 512 * (tc + 1))
                g.wait_ge(S["sA"], u + 1)
                g.dma_start(out=a_dram[128 * iP:128 * (iP + 1), cs],
                            in_=a_sb[u % 2][:]).then_inc(S[f"sSPA{u % 2}"], 16)
                g.wait_ge(S["sW"], u + 1)
                g.dma_start(out=w_dram[128 * iP:128 * (iP + 1), cs],
                            in_=w_sb[u % 2][:]).then_inc(S[f"sSPW{u % 2}"], 16)
                g.wait_ge(S["sZ"], u + 1)
                g.dma_start(out=z_dram[128 * iP:128 * (iP + 1), cs],
                            in_=z_sb[u % 2][:]).then_inc(S[f"sSPZ{u % 2}"], 16)

        def rstores(rts):
            for rt in rts:
                g.wait_ge(S["sWR"], rt + 1)
                g.dma_start(out=wr_out[128 * rt:128 * (rt + 1), :],
                            in_=wr_sb[rt % 2][:, 0:32]).then_inc(S[f"sSWR{rt % 2}"], 16)
                g.dma_start(out=f_out[128 * rt:128 * (rt + 1), :],
                            in_=f_sb[rt % 2][:]).then_inc(S[f"sSWR{rt % 2}"], 16)
        for P in range(16):
            if P >= 1:
                spills(P - 1)
            if P == 8:
                rstores(range(16))
            g.wait_ge(S[f"sLW{P % 2}"], 32 * (P // 2 + 1))
            if P >= 2:
                g.wait_ge(S["sPE"], 4 * (P - 1))
            g.tensor_copy(wb[P % 2][:], w32[P % 2][:]).then_inc(S["sWC"], 1)
            if P == 7:
                g.wait_ge(S["sLR"], 16)
                g.tensor_copy(wcb[:], wc32[:]).then_inc(S["sWCR"], 1)
        spills(15)
        rstores(range(16, 32))
        for i in range(8):
            g.wait_ge(S["sHB"], i + 1)
            g.dma_start(out=h_out[128 * i:128 * (i + 1), :],
                        in_=hb_sb[i % 2][:]).then_inc(S[f"sSH{i % 2}"], 16)
        for q in range(2):
            g.wait_ge(S[f"sSH{q}"], 16 * 4)
            g.wait_ge(S[f"sSWR{q}"], 16 * 32)
    return nc


def make_k1():
    from contextlib import ExitStack
    nc = bass.Bass("TRN2", target_bir_lowering=False)
    with ExitStack() as ctx:
        build_k1(nc, ctx.enter_context)
    return nc


def k1_inputs(x, W_in, w_cls, bias, a_param, core):
    import numpy as np
    import ml_dtypes
    b, H = core // 2, core % 2
    xb = np.asarray(x[b])
    xroll = np.ascontiguousarray(np.roll(xb, -1024 * H, axis=1))
    WT = np.asarray(W_in).T
    WTr = np.roll(WT, -1024 * H, axis=0)
    winT = np.ascontiguousarray(np.concatenate(
        [WTr[:, 1024 * H:1024 * (H + 1)],
         WTr[:, 2048 + 1024 * H:2048 + 1024 * (H + 1)]], axis=1))
    wclsT = np.ascontiguousarray(np.roll(np.asarray(w_cls).T, -1024 * H, axis=0))
    biasb = np.ascontiguousarray(np.tile(np.asarray(bias)[None, :], (128, 1)))
    apar = np.ascontiguousarray(np.asarray(a_param)[:, None])
    ident = np.eye(128, dtype=ml_dtypes.bfloat16)
    return dict(xroll=xroll.astype(np.float32), winT=winT.astype(np.float32),
                wclsT=wclsT.astype(np.float32), biasb=biasb.astype(np.float32),
                apar=apar.astype(np.float32), ident=ident)



"""K2: expert-sharded MoE compute. Core c owns experts 4c..4c+3. See NOTES.md."""
import os
import concourse.bass as bass
import concourse.mybir as mybir
from concourse import library_config as libcfg
K2DBG = False

F32 = mybir.dt.float32
BF16 = mybir.dt.bfloat16
I16 = mybir.dt.int16
I32 = mybir.dt.int32
U32 = mybir.dt.uint32
AF = mybir.ActivationFunctionType
ALU = mybir.AluOpType
AX = mybir.AxisListType.X


def build_k2(nc: bass.Bass, E):
    x_full = nc.dram_tensor("x_full", [16384, 2048], F32, kind="ExternalInput").ap()
    rout = nc.dram_tensor("rout", [16384, 4], F32, kind="ExternalInput").ap()
    wgu = nc.dram_tensor("wgu", [4, 2048, 2048], F32, kind="ExternalInput").ap()
    wdn = nc.dram_tensor("wdn", [4, 1024, 2048], F32, kind="ExternalInput").ap()
    ident = nc.dram_tensor("ident", [128, 128], BF16, kind="ExternalInput").ap()
    triu = nc.dram_tensor("triu", [128, 128], BF16, kind="ExternalInput").ap()
    ytab = nc.dram_tensor("ytab", [4096, 2048], F32, kind="ExternalOutput").ap()
    if K2DBG:
        d_idcq = nc.dram_tensor("d_idcq", [512, 64], I16, kind="ExternalOutput").ap()
        d_nfc = nc.dram_tensor("d_nfc", [1, 8], U32, kind="ExternalOutput").ap()
        d_wr = nc.dram_tensor("d_wr", [512, 8], F32, kind="ExternalOutput").ap()
        d_xg = nc.dram_tensor("d_xg", [512, 2048], F32, kind="ExternalOutput").ap()
        d_buf = nc.dram_tensor("d_buf", [128, 16384], BF16, kind="ExternalOutput").ap()
        d_act = nc.dram_tensor("d_act", [128, 8192], BF16, kind="ExternalOutput").ap()

    cid_d = nc.dram_tensor("cid_d", [4, 16384], F32, kind="Internal").ap()
    wrc_d = nc.dram_tensor("wrc_d", [4, 16384], F32, kind="Internal").ap()
    idq_d = nc.dram_tensor("idq_d", [4, 16, 64], I16, kind="Internal").ap()
    wrq_d = nc.dram_tensor("wrq_d", [4, 1024], F32, kind="Internal").ap()
    cnt_d = nc.dram_tensor("cnt_d", [32], F32, kind="Internal").ap()
    cnt_d2 = nc.dram_tensor("cnt_d2", [16], F32, kind="Internal").ap()

    s_id = E(nc.sbuf_tensor("s_id", [128, 128], BF16))
    s_tri = E(nc.sbuf_tensor("s_tri", [128, 128], BF16))
    idcq = [E(nc.sbuf_tensor(f"idcq{k}", [128, 64], I16)) for k in range(4)]
    idc16 = E(nc.sbuf_tensor("idc16", [16, 64], I16))
    nfe = [E(nc.sbuf_tensor(f"nfe{k}", [1, 1], U32)) for k in range(4)]
    nfx = E(nc.sbuf_tensor("nfx", [1, 1], U32))
    nf_f = E(nc.sbuf_tensor("nf_f", [1, 16], F32))
    nfc = E(nc.sbuf_tensor("nfc", [1, 8], U32))
    wr128 = [E(nc.sbuf_tensor(f"wr128_{k}", [128, 8], F32)) for k in range(4)]
    stc = [E(nc.sbuf_tensor(f"stc{k}", [128, 8], F32)) for k in range(2)]
    idc_f = E(nc.sbuf_tensor("idc_f", [16, 64], F32))
    wrs_f = E(nc.sbuf_tensor("wrs_f", [16, 64], F32))
    slab_c = E(nc.sbuf_tensor("slab_c", [16, 1024], F32))
    slab_w = E(nc.sbuf_tensor("slab_w", [16, 1024], F32))
    mid_c = E(nc.sbuf_tensor("mid_c", [16, 128], F32))
    mid_w = E(nc.sbuf_tensor("mid_w", [16, 128], F32))
    nfa = E(nc.sbuf_tensor("nfa", [1, 1], U32))
    nfb = E(nc.sbuf_tensor("nfb", [1, 1], U32))
    ones16 = E(nc.sbuf_tensor("ones16", [1, 32], F32))
    cnt_row = E(nc.sbuf_tensor("cnt_row", [1, 32], F32))
    cnt16 = E(nc.sbuf_tensor("cnt16", [16, 2], F32))
    cnt16f = E(nc.sbuf_tensor("cnt16f", [16, 1], F32))
    io64i = E(nc.sbuf_tensor("io64i", [16, 64], I32))
    io64 = E(nc.sbuf_tensor("io64", [16, 64], F32))
    mska = E(nc.sbuf_tensor("mska", [16, 64], F32))
    mskb = E(nc.sbuf_tensor("mskb", [16, 64], F32))
    tmsk = E(nc.sbuf_tensor("tmsk", [16, 64], F32))
    tid_i = E(nc.sbuf_tensor("tid_i", [128, 128], I32))
    sq_scr = E(nc.sbuf_tensor("sq_scr", [128, 2048], F32))

    arena = E(nc.sbuf_tensor("arena", [128, 47104], F32))
    ar32 = arena
    ar16 = arena.bitcast(BF16)

    def cv32(off, cols, shape=None):
        ap = ar32[:, off // 4: off // 4 + cols]
        if shape is None:
            return ap
        pat = " ".join(chr(97 + i) for i in range(len(shape)))
        return ap.rearrange(f"p ({pat}) -> p {pat}",
                            **{chr(97 + i): s for i, s in enumerate(shape)})

    def cv16(off, cols, shape=None):
        ap = ar16[:, off // 2: off // 2 + cols]
        if shape is None:
            return ap
        pat = " ".join(chr(97 + i) for i in range(len(shape)))
        return ap.rearrange(f"p ({pat}) -> p {pat}",
                            **{chr(97 + i): s for i, s in enumerate(shape)})

    routS = cv32(0, 512, (128, 4))
    A_fe = cv32(2048, 512, (128, 4))
    posb = cv32(4096, 512, (4, 128))
    keepb = cv32(6144, 512, (4, 128))
    cida = cv32(8192, 512, (4, 128))
    wrca = cv32(10240, 512, (4, 128))
    tid_f = cv32(12288, 128)
    tidp1 = cv32(12800, 128)
    zeros128 = cv32(13312, 128)
    rs32 = cv32(13824, 4)
    rs16 = cv16(13856, 4)
    pref = cv32(13888, 4)
    xg = cv32(0, 8192, (4, 2048))
    xng = [cv16(32768 + 4096 * k, 2048) for k in range(2)]
    buf_fm = cv16(40960, 16384, (16, 1024))
    sg_sb = cv16(73728, 8192, (8, 1024))
    act_sb = cv16(90112, 8192, (8, 1024))
    wgur = cv16(106496, 16384, (16, 1024))
    wstg = [cv32(172032 + 8192 * k, 2048) for k in range(2)]
    wdnr = cv16(139264, 16384, (8, 2048))
    ysb = [cv32(73728 + 8192 * k, 2048) for k in range(2)]

    tr = [E(nc.psum_tensor(f"tr{k}", [128, 1024], BF16)) for k in range(2)]
    pgu = [E(nc.psum_tensor(f"pgu{k}", [128, 512], F32)) for k in range(2)]
    pdn = [E(nc.psum_tensor(f"pdn{k}", [128, 512], F32)) for k in range(2)]
    ppre = E(nc.psum_tensor("ppre", [128, 4], F32))

    sem_names = ["sLD", "sIOTA", "sRS16", "sPRE", "sPREV", "sCID",
                 "sBO0", "sBO1", "sSLC", "sSLW", "sMS", "sSG", "sCV", "sCNT",
                 "sIDW0", "sIDW1", "sIDW2", "sIDW3", "sIDR0", "sIDR1", "sIDR2", "sIDR3", "sWRW0", "sWRW1", "sWRW2", "sWRW3", "sWRL0", "sWRL1", "sWRL2", "sWRL3",
                 "sGA0", "sGA1", "sSQ2", "sVS2", "sRS2", "sNB2", "sXN2",
                 "sTP2", "sTE2",
                 "sWG0", "sWG1", "sWGC", "sWD0", "sWD1", "sWDC",
                 "sGU", "sSIL", "sACT", "sDN", "sYS", "sYW0", "sYW1",
                 "sSS", "sSV", "sXGZ", "sGP", "sS1", "sCR", "sCL", "sMK", "sCR2", "sCL2"]
    S = {n: E(nc.semaphore(n)) for n in sem_names}
    block = E(nc.Block())

    @block.sync
    def _(sync):
        sync.dma_start(out=s_id[:], in_=ident).then_inc(S["sLD"], 16)
        sync.dma_start(out=s_tri[:], in_=triu).then_inc(S["sLD"], 16)
        sync.dma_start(out=routS[:],
                       in_=rout.rearrange("(p f) e -> p f e", p=128)
                       ).then_inc(S["sLD"], 16)
        sync.wait_ge(S["sCID"], 1)
        sync.dma_start(out=cid_d.rearrange("e (p f) -> p e f", p=128),
                       in_=cida[:]).then_inc(S["sBO0"], 16)
        sync.dma_start(out=wrc_d.rearrange("e (p f) -> p e f", p=128),
                       in_=wrca[:]).then_inc(S["sBO1"], 16)
        for e in range(4):
            if e == 0:
                sync.wait_ge(S["sBO0"], 16)
                sync.wait_ge(S["sBO1"], 16)
            else:
                sync.wait_ge(S["sSG"], 2 * e)
            sync.dma_start(out=slab_c[:], in_=cid_d[e].rearrange("(f p) -> p f", p=16),
                           allow_slow_non_contiguous=True).then_inc(S["sSLC"], 16)
            sync.dma_start(out=slab_w[:], in_=wrc_d[e].rearrange("(f p) -> p f", p=16),
                           allow_slow_non_contiguous=True).then_inc(S["sSLW"], 16)
            sync.wait_ge(S["sCR"], e + 1)
            sync.dma_start(out=cnt_d, in_=cnt_row[:]).then_inc(S["sCL"], 16)
            sync.wait_ge(S["sCL"], 32 * e + 16)
            sync.dma_start(out=cnt16[:],
                           in_=cnt_d.rearrange("(c p) -> p c", p=16),
                           allow_slow_non_contiguous=True).then_inc(S["sCL"], 16)
            sync.wait_ge(S["sCR2"], e + 1)
            sync.dma_start(out=cnt_d2, in_=cnt_row[0:1, 0:16]).then_inc(S["sCL2"], 16)
            sync.wait_ge(S["sCL2"], 32 * e + 16)
            sync.dma_start(out=cnt16f[:],
                           in_=cnt_d2.rearrange("(c p) -> p c", p=16)[:, 0:1],
                           allow_slow_non_contiguous=True).then_inc(S["sCL2"], 16)
            sync.wait_ge(S["sCV"], e + 1)
            sync.dma_start(out=idq_d[e], in_=idc16[:]).then_inc(S[f"sIDW{e}"], 16)
            sync.dma_start(out=wrq_d[e].rearrange("(f p) -> p f", p=16),
                           in_=wrs_f[:],
                           allow_slow_non_contiguous=True).then_inc(S[f"sWRW{e}"], 16)
            sync.wait_ge(S[f"sIDW{e}"], 16)
            for gq in range(8):
                sync.dma_start(out=idcq[e][16 * gq:16 * (gq + 1), :],
                               in_=idq_d[e]).then_inc(S[f"sIDR{e}"], 16)
        for k in range(4):
            sync.wait_ge(S[f"sWRW{k}"], 16)
            sync.dma_start(out=wr128[k][:],
                           in_=wrq_d[k].rearrange("(rb p) -> p rb", p=128),
                           allow_slow_non_contiguous=True).then_inc(S[f"sWRL{k}"], 16)
        for k in range(4):
            for fh in range(2):
                for hc in range(16):
                    n = (k * 2 + fh) * 16 + hc
                    if n >= 2:
                        sync.wait_ge(S["sWGC"], n - 1)
                    if fh == 0 and hc == 0 and k > 0:
                        sync.wait_ge(S["sWDC"], 8 * k)
                    sync.dma_start(
                        out=wstg[n % 2][:, 0:1024],
                        in_=wgu[k][128 * hc:128 * (hc + 1),
                                   1024 * fh:1024 * (fh + 1)]
                    ).then_inc(S[f"sWG{n % 2}"], 16)
            for ci in range(8):
                n = k * 8 + ci
                if n >= 2:
                    sync.wait_ge(S["sWDC"], n - 1)
                if ci == 0:
                    sync.wait_ge(S["sWGC"], 32 * (k + 1))
                sync.dma_start(out=wstg[n % 2][:],
                               in_=wdn[k][128 * ci:128 * (ci + 1), :]
                               ).then_inc(S[f"sWD{n % 2}"], 16)
        sync.wait_ge(S["sYW0"], 16 * 16)
        sync.wait_ge(S["sYW1"], 16 * 16)
        if K2DBG:
            for k in range(4):
                sync.dma_start(out=d_idcq[128 * k:128 * (k + 1), :],
                               in_=idcq[k][:]).then_inc(S["sLD"], 16)
                sync.dma_start(out=d_wr[128 * k:128 * (k + 1), :],
                               in_=wr128[k][:]).then_inc(S["sLD"], 16)
            sync.dma_start(out=d_nfc, in_=nfc[:]).then_inc(S["sLD"], 16)
            sync.dma_start(out=d_xg.rearrange("(g p) f -> p g f", p=128),
                           in_=xg[:]).then_inc(S["sLD"], 16)
            sync.dma_start(out=d_buf, in_=buf_fm[:].rearrange("p a b -> p (a b)")
                           ).then_inc(S["sLD"], 16)
            sync.dma_start(out=d_act, in_=act_sb[:].rearrange("p a b -> p (a b)")
                           ).then_inc(S["sLD"], 16)
            sync.wait_ge(S["sLD"], 48 + 16 * 12)

    @block.gpsimd
    def _(g):
        g.iota(tid_i[:], pattern=[[1, 128]], base=0, channel_multiplier=128
               ).then_inc(S["sIOTA"], 1)
        g.iota(io64i[:], pattern=[[16, 64]], base=0, channel_multiplier=1
               ).then_inc(S["sIOTA"], 1)
        g.load_library(libcfg.sparse_gather)
        for e in range(4):
            g.wait_ge(S["sSLC"], 16 * (e + 1))
            g.wait_ge(S["sSLW"], 16 * (e + 1))
            g.wait_ge(S["sMS"], e + 1)
            base = 6 * e
            g.sparse_gather(mid_c[0:16, 0:64], slab_c[0:16, 0:512],
                            num_found=nfa[:]).then_inc(S["sGP"], 1)
            g.wait_ge(S["sGP"], base + 1)
            g.sparse_gather(mid_c[0:16, 64:128], slab_c[0:16, 512:1024],
                            num_found=nfb[:]).then_inc(S["sGP"], 1)
            g.wait_ge(S["sGP"], base + 2)
            g.sparse_gather(mid_w[0:16, 0:64], slab_w[0:16, 0:512],
                            num_found=nfx[:]).then_inc(S["sGP"], 1)
            g.wait_ge(S["sGP"], base + 3)
            g.sparse_gather(mid_w[0:16, 64:128], slab_w[0:16, 512:1024],
                            num_found=nfx[:]).then_inc(S["sGP"], 1)
            g.wait_ge(S["sGP"], base + 4)
            g.engine_nop().then_inc(S["sS1"], 1)
            g.wait_ge(S["sMK"], e + 1)
            g.sparse_gather(idc_f[:], mid_c[:], num_found=nfe[e][:]
                            ).then_inc(S["sGP"], 1)
            g.wait_ge(S["sGP"], base + 5)
            g.sparse_gather(wrs_f[:], mid_w[:], num_found=nfx[:]
                            ).then_inc(S["sGP"], 1)
            g.wait_ge(S["sGP"], base + 6)
            g.engine_nop().then_inc(S["sSG"], 2)
        g.load_library(libcfg.mlp)
        for k in range(4):
            g.wait_ge(S["sCNT"], k + 1)
            c0 = g.value_load(nfc[0:1, 2 * k:2 * k + 1])
            c1 = g.value_load(nfc[0:1, 2 * k + 1:2 * k + 2])
            g.wait_ge(S[f"sIDR{k}"], 16 * 8)
            if k == 0:
                g.wait_ge(S["sBO0"], 16)
                g.wait_ge(S["sBO1"], 16)
                g.wait_ge(S["sSLC"], 64)
                g.wait_ge(S["sSLW"], 64)
                g.wait_ge(S["sXGZ"], 1)
            for half, creg in ((0, c0), (1, c1)):
                G0 = (k * 2 + half) * 4
                if G0 > 0:
                    g.wait_ge(S["sXN2"], G0)
                g.dma_gather(xg[:], x_full, idcq[k][:, 32 * half:32 * (half + 1)],
                             512, creg, 2048).then_inc(S[f"sGA{half}"], 16)
            for r in range(8):
                g.wait_ge(S["sYS"], 32 * k + 4 * (r + 1))
                g.dma_start(out=ytab[1024 * k + 128 * r:1024 * k + 128 * (r + 1), :],
                            in_=ysb[r % 2][:]).then_inc(S[f"sYW{r % 2}"], 16)

    @block.scalar
    def _(scalar):
        cS = [0]
        def sinc(ins):
            cS[0] += 1
            ins.then_inc(S["sSS"], 1)
        def swait():
            scalar.wait_ge(S["sSS"], cS[0])
        for k in range(4):
            for half in range(2):
                for gg in range(4):
                    G = (k * 2 + half) * 4 + gg
                    p = G % 2
                    scalar.wait_ge(S[f"sGA{half}"], 16 * (k + 1))
                    if G >= 2:
                        scalar.wait_ge(S["sVS2"], G - 1)
                    scalar.activation(sq_scr[:], xg[:, gg, :], AF.Square,
                                      accum_out=stc[p][:, 0:1]).then_inc(S["sSQ2"], 1)
                    scalar.wait_ge(S["sVS2"], G + 1)
                    sinc(scalar.activation(stc[p][:, 4:5], stc[p][:, 3:4], AF.Ln))
                    swait()
                    scalar.activation(stc[p][:, 5:6], stc[p][:, 4:5], AF.Exp,
                                      scale=-0.5).then_inc(S["sRS2"], 1)
                    scalar.wait_ge(S["sNB2"], G + 1)
                    if G >= 2:
                        scalar.wait_ge(S["sTP2"], 2 * G - 2)
                    scalar.activation(xng[p][:], xg[:, gg, :], AF.Identity,
                                      bias=stc[p][:, 6:7], scale=stc[p][:, 5:6]
                                      ).then_inc(S["sXN2"], 1)
            for u in range(16):
                scalar.wait_ge(S["sGU"], 32 * k + u + 1)
                if u == 0 and k > 0:
                    scalar.wait_ge(S["sYW0"], 16 * 4 * k)
                    scalar.wait_ge(S["sYW1"], 16 * 4 * k)
                r2, fi = u // 8, u % 8
                scalar.activation(sg_sb[:, fi, 512 * r2:512 * (r2 + 1)],
                                  pgu[u % 2][:], AF.Silu).then_inc(S["sSIL"], 1)
            for u in range(32):
                r, hc = u // 4, u % 4
                scalar.wait_ge(S["sDN"], 32 * k + u + 1)
                scalar.wait_ge(S[f"sWRL{k}"], 16)
                if hc == 0:
                    scalar.wait_ge(S[f"sYW{r % 2}"], 16 * (k * 4 + r // 2))
                scalar.mul(ysb[r % 2][:, 512 * hc:512 * (hc + 1)],
                           pdn[u % 2][:], wr128[k][:, r:r + 1]).then_inc(S["sYS"], 1)

    @block.vector
    def _(vector):
        cV = [0]
        def vinc(ins):
            cV[0] += 1
            ins.then_inc(S["sSV"], 1)
        def vwait():
            vector.wait_ge(S["sSV"], cV[0])
        vector.wait_ge(S["sIOTA"], 2)
        vinc(vector.tensor_copy(tid_f[:], tid_i[:]))
        vwait()
        vinc(vector.tensor_scalar_add(tidp1[:], tid_f[:], 1.0))
        vector.memset(zeros128[:], 0.0)
        vector.wait_ge(S["sLD"], 48)
        vinc(vector.tensor_single_scalar(A_fe[:], routS[:], 0.0, ALU.is_gt))
        vwait()
        vinc(vector.tensor_reduce(rs32[:], A_fe[:].rearrange("p f e -> p e f"),
                                  axis=AX, op=ALU.add))
        vwait()
        vector.tensor_copy(rs16[:], rs32[:]).then_inc(S["sRS16"], 1)
        vector.wait_ge(S["sPRE"], 1)
        vinc(vector.tensor_copy(pref[:], ppre[:]))
        vwait()
        for e in range(4):
            vinc(vector.tensor_tensor_scan(
                posb[:, e, :], A_fe[:, :, e], zeros128[:],
                pref[:, e:e + 1], ALU.add, ALU.add))
        vwait()
        vinc(vector.tensor_sub(posb[:], posb[:],
                               A_fe[:].rearrange("p f e -> p e f")))
        vwait()
        vinc(vector.tensor_single_scalar(keepb[:], posb[:], 1024.0, ALU.is_lt))
        vwait()
        vinc(vector.tensor_mul(keepb[:], keepb[:],
                               A_fe[:].rearrange("p f e -> p e f")))
        vwait()
        for e in range(4):
            vinc(vector.tensor_mul(cida[:, e, :], keepb[:, e, :], tidp1[:]))
            vinc(vector.tensor_mul(wrca[:, e, :], keepb[:, e, :], routS[:, :, e]))
        vwait()
        vinc(vector.tensor_scalar_add(cida[:].rearrange("p a b -> p (a b)"),
                                      cida[:].rearrange("p a b -> p (a b)"), -1.0))
        vwait()
        vinc(vector.tensor_add(wrca[:], wrca[:], keepb[:]))
        vwait()
        vector.tensor_scalar_add(wrca[:].rearrange("p a b -> p (a b)"),
                                 wrca[:].rearrange("p a b -> p (a b)"), -1.0
                                 ).then_inc(S["sCID"], 1)
        vinc(vector.memset(ones16[:], 1.0))
        vector.wait_ge(S["sIOTA"], 2)
        vinc(vector.tensor_copy(io64[:], io64i[:]))
        for e in range(4):
            if e > 0:
                vector.wait_ge(S[f"sIDW{e - 1}"], 16)
                vector.wait_ge(S[f"sWRW{e - 1}"], 16)
            vinc(vector.memset(idc_f[:], -1.0))
            vwait()
            vinc(vector.memset(mid_c[:], -1.0))
            vwait()
            vinc(vector.memset(mid_w[:], -1.0))
            vwait()
            vector.memset(wrs_f[:], 0.0).then_inc(S["sMS"], 1)
            vector.wait_ge(S["sS1"], e + 1)
            vinc(vector.tensor_copy(nf_f[0:1, 0:1], nfa[:]))
            vinc(vector.tensor_copy(nf_f[0:1, 1:2], nfb[:]))
            vwait()
            vinc(vector.tensor_single_scalar(cnt_row[0:1, 0:16], ones16[0:1, 0:16],
                                             nf_f[0:1, 0:1], ALU.mult))
            vwait()
            vector.tensor_single_scalar(cnt_row[0:1, 16:32], ones16[0:1, 0:16],
                                        nf_f[0:1, 1:2], ALU.mult
                                        ).then_inc(S["sCR"], 1)
            vector.wait_ge(S["sCL"], 32 * e + 32)
            vinc(vector.tensor_single_scalar(mska[:], io64[:], cnt16[:, 0:1],
                                             ALU.is_lt))
            vwait()
            vinc(vector.tensor_single_scalar(mskb[:], io64[:], cnt16[:, 1:2],
                                             ALU.is_lt))
            vwait()
            vinc(vector.tensor_scalar_add(tmsk[:], mid_c[0:16, 0:64], 1.0))
            vwait()
            vinc(vector.tensor_mul(tmsk[:], tmsk[:], mska[:]))
            vwait()
            vinc(vector.tensor_scalar_add(mid_c[0:16, 0:64], tmsk[:], -1.0))
            vwait()
            vinc(vector.tensor_scalar_add(tmsk[:], mid_c[0:16, 64:128], 1.0))
            vwait()
            vinc(vector.tensor_mul(tmsk[:], tmsk[:], mskb[:]))
            vwait()
            vinc(vector.tensor_scalar_add(mid_c[0:16, 64:128], tmsk[:], -1.0))
            vwait()
            vinc(vector.tensor_scalar_add(tmsk[:], mid_w[0:16, 0:64], 1.0))
            vwait()
            vinc(vector.tensor_mul(tmsk[:], tmsk[:], mska[:]))
            vwait()
            vinc(vector.tensor_scalar_add(mid_w[0:16, 0:64], tmsk[:], -1.0))
            vwait()
            vinc(vector.tensor_scalar_add(tmsk[:], mid_w[0:16, 64:128], 1.0))
            vwait()
            vinc(vector.tensor_mul(tmsk[:], tmsk[:], mskb[:]))
            vwait()
            vinc(vector.tensor_scalar_add(mid_w[0:16, 64:128], tmsk[:], -1.0))
            vwait()
            vector.engine_nop().then_inc(S["sMK"], 1)
            vector.wait_ge(S["sSG"], 2 * (e + 1))
            vinc(vector.tensor_copy(nf_f[0:1, 2:3], nfe[e][:]))
            vwait()
            vector.tensor_single_scalar(cnt_row[0:1, 0:16], ones16[0:1, 0:16],
                                        nf_f[0:1, 2:3], ALU.mult
                                        ).then_inc(S["sCR2"], 1)
            vector.wait_ge(S["sCL2"], 32 * (e + 1))
            vinc(vector.tensor_single_scalar(mska[:], io64[:], cnt16f[:, 0:1],
                                             ALU.is_lt))
            vwait()
            vinc(vector.tensor_scalar_add(tmsk[:], idc_f[:], 1.0))
            vwait()
            vinc(vector.tensor_mul(tmsk[:], tmsk[:], mska[:]))
            vwait()
            vinc(vector.tensor_scalar_add(idc_f[:], tmsk[:], -1.0))
            vwait()
            vinc(vector.tensor_mul(wrs_f[:], wrs_f[:], mska[:]))
            vwait()
            vinc(vector.tensor_scalar_max(idc_f[0:1, 0:1], idc_f[0:1, 0:1], 0.0))
            vwait()
            vinc(vector.tensor_single_scalar(idc_f[0:1, 0:1], idc_f[0:1, 0:1],
                                             16383.0, ALU.min))
            vwait()
            vinc(vector.tensor_scalar_max(idc_f[0:1, 32:33], idc_f[0:1, 32:33], 0.0))
            vwait()
            vinc(vector.tensor_single_scalar(idc_f[0:1, 32:33], idc_f[0:1, 32:33],
                                             16383.0, ALU.min))
            vwait()
            vinc(vector.tensor_scalar_max(wrs_f[:], wrs_f[:], 0.0))
            vwait()
            vinc(vector.tensor_copy(idc16[:], idc_f[:]))
            vinc(vector.tensor_copy(nf_f[0:1, 2 * e + 4:2 * e + 5], nf_f[0:1, 2:3]))
            vwait()
            co = 2 * e + 4
            vinc(vector.tensor_single_scalar(nf_f[0:1, co + 1:co + 2],
                                             nf_f[0:1, co:co + 1], 512.0,
                                             ALU.subtract))
            vwait()
            vinc(vector.tensor_single_scalar(nf_f[0:1, co:co + 1],
                                             nf_f[0:1, co:co + 1], 512.0,
                                             ALU.min))
            vwait()
            vinc(vector.tensor_single_scalar(nf_f[0:1, co:co + 2],
                                             nf_f[0:1, co:co + 2], 1.0,
                                             ALU.max))
            vwait()
            vinc(vector.tensor_single_scalar(nf_f[0:1, co + 1:co + 2],
                                             nf_f[0:1, co + 1:co + 2], 512.0,
                                             ALU.min))
            vwait()
            vinc(vector.tensor_copy(nfc[0:1, 2 * e:2 * e + 2],
                                    nf_f[0:1, co:co + 2]))
            vwait()
            vector.engine_nop().then_inc(S["sCV"], 1)
            vector.engine_nop().then_inc(S["sCNT"], 1)
        vector.wait_ge(S["sBO0"], 16)
        vector.wait_ge(S["sBO1"], 16)
        vector.memset(xg[:], 0.0).then_inc(S["sXGZ"], 1)
        for k in range(4):
            for half in range(2):
                for gg in range(4):
                    G = (k * 2 + half) * 4 + gg
                    p = G % 2
                    vector.wait_ge(S[f"sGA{half}"], 16 * (k + 1))
                    if G >= 2:
                        vector.wait_ge(S["sXN2"], G - 1)
                    vinc(vector.reduce_sum(stc[p][:, 1:2], xg[:, gg, :], axis=AX,
                                           negate=True))
                    vwait()
                    vinc(vector.tensor_scalar_mul(stc[p][:, 1:2], stc[p][:, 1:2],
                                                  1.0 / 2048.0))
                    vector.wait_ge(S["sSQ2"], G + 1)
                    vinc(vector.tensor_scalar_mul(stc[p][:, 2:3], stc[p][:, 0:1],
                                                  1.0 / 2048.0))
                    vwait()
                    vinc(vector.tensor_mul(stc[p][:, 3:4], stc[p][:, 1:2],
                                           stc[p][:, 1:2]))
                    vwait()
                    vinc(vector.tensor_sub(stc[p][:, 3:4], stc[p][:, 2:3],
                                           stc[p][:, 3:4]))
                    vwait()
                    vector.tensor_scalar_add(stc[p][:, 3:4], stc[p][:, 3:4], 1e-5
                                             ).then_inc(S["sVS2"], 1)
                    vector.wait_ge(S["sRS2"], G + 1)
                    vwait()
                    vector.tensor_mul(stc[p][:, 6:7], stc[p][:, 1:2],
                                      stc[p][:, 5:6]).then_inc(S["sNB2"], 1)
                    for g2 in range(2):
                        tg = 2 * G + g2
                        vector.wait_ge(S["sTP2"], tg + 1)
                        vector.tensor_copy(
                            buf_fm[:, 8 * g2:8 * (g2 + 1),
                                   512 * half + 128 * gg:512 * half + 128 * (gg + 1)],
                            tr[tg % 2][:].rearrange("p (j t2) -> p j t2", t2=128),
                        ).then_inc(S["sTE2"], 1)
            for fh in range(2):
                for hc in range(16):
                    n = (k * 2 + fh) * 16 + hc
                    vector.wait_ge(S[f"sWG{n % 2}"], 16 * (n // 2 + 1))
                    vector.wait_ge(S["sGU"], 32 * k + 16 * fh)
                    vector.tensor_copy(wgur[:, hc, :], wstg[n % 2][:, 0:1024]
                                       ).then_inc(S["sWGC"], 1)
            for u in range(16):
                r2, fi = u // 8, u % 8
                vector.wait_ge(S["sGU"], 32 * k + 16 + u + 1)
                vector.wait_ge(S["sSIL"], 16 * (k + 1))
                vector.tensor_mul(act_sb[:, fi, 512 * r2:512 * (r2 + 1)],
                                  sg_sb[:, fi, 512 * r2:512 * (r2 + 1)],
                                  pgu[u % 2][:]).then_inc(S["sACT"], 1)
            for ci in range(8):
                n = k * 8 + ci
                vector.wait_ge(S[f"sWD{n % 2}"], 16 * (n // 2 + 1))
                vector.wait_ge(S["sDN"], 32 * k)
                vector.tensor_copy(wdnr[:, ci, :], wstg[n % 2][:]
                                   ).then_inc(S["sWDC"], 1)

    @block.tensor
    def _(tensor):
        tensor.wait_ge(S["sLD"], 48)
        tensor.wait_ge(S["sRS16"], 1)
        tensor.matmul(ppre[:], s_tri[:], rs16[:], start=True, stop=True
                      ).then_inc(S["sPRE"], 1)
        for k in range(4):
            for half in range(2):
                for gg in range(4):
                    G = (k * 2 + half) * 4 + gg
                    tensor.wait_ge(S["sXN2"], G + 1)
                    for g2 in range(2):
                        tg = 2 * G + g2
                        if tg >= 2:
                            tensor.wait_ge(S["sTE2"], tg - 1)
                        for j in range(8):
                            jj = 8 * g2 + j
                            ins = tensor.transpose(
                                tr[tg % 2][:, 128 * j:128 * (j + 1)],
                                xng[G % 2][:, 128 * jj:128 * (jj + 1)],
                                s_id[:])
                        ins.then_inc(S["sTP2"], 1)
            tensor.wait_ge(S["sTE2"], 16 * (k + 1))
            def gu_bank_free(U):
                if U < 0:
                    return
                kp, up = divmod(U, 32)
                if up < 16:
                    tensor.wait_ge(S["sSIL"], 16 * kp + up + 1)
                else:
                    tensor.wait_ge(S["sACT"], 16 * kp + (up - 16) + 1)
            for fh in range(2):
                tensor.wait_ge(S["sWGC"], (2 * k + fh + 1) * 16)
                for u in range(16):
                    r2, fi = u // 8, u % 8
                    uu = 16 * fh + u
                    gu_bank_free(32 * k + uu - 2)
                    for hc in range(16):
                        ins = tensor.matmul(
                            pgu[uu % 2][:],
                            wgur[:, hc, 128 * fi:128 * (fi + 1)],
                            buf_fm[:, hc, 512 * r2:512 * (r2 + 1)],
                            start=(hc == 0), stop=(hc == 15))
                    ins.then_inc(S["sGU"], 1)
            tensor.wait_ge(S["sACT"], 16 * (k + 1))
            tensor.wait_ge(S["sWDC"], 8 * (k + 1))
            for u in range(32):
                r, hc = u // 4, u % 4
                if 32 * k + u - 1 > 0:
                    tensor.wait_ge(S["sYS"], 32 * k + u - 1)
                for ci in range(8):
                    ins = tensor.matmul(
                        pdn[u % 2][:],
                        act_sb[:, ci, 128 * r:128 * (r + 1)],
                        wdnr[:, ci, 512 * hc:512 * (hc + 1)],
                        start=(ci == 0), stop=(ci == 7))
                ins.then_inc(S["sDN"], 1)
    return nc


def make_k2():
    from contextlib import ExitStack
    nc = bass.Bass("TRN2", target_bir_lowering=False)
    with ExitStack() as ctx:
        build_k2(nc, ctx.enter_context)
    from concourse.library_overlay import lower_extended_insts
    lower_extended_insts(nc)
    return nc


def k2_inputs(x_flat, wr_full, w_gu, w_dn, core):
    import numpy as np
    import ml_dtypes
    return dict(
        x_full=np.ascontiguousarray(x_flat.astype(np.float32)),
        rout=np.ascontiguousarray(wr_full[:, 4 * core:4 * (core + 1)].astype(np.float32)),
        wgu=np.ascontiguousarray(w_gu[4 * core:4 * (core + 1)].astype(np.float32)),
        wdn=np.ascontiguousarray(w_dn[4 * core:4 * (core + 1)].astype(np.float32)),
        ident=np.eye(128, dtype=ml_dtypes.bfloat16),
        triu=np.triu(np.ones((128, 128), np.float32), 1).astype(ml_dtypes.bfloat16),
    )



"""K3 v2: range-based combine. Core c owns tokens [2048c, +2048).

out[t] = x[t] + h[t] + sum over kept routes (t, e) of ytab[slot(t, e)]
where ytab rows are already scaled by w_r * f (K1 folds f into wr, K2
scales y rows). Slots of expert e serving own tokens form the contiguous
ytab row range [1024e + lo_e, 1024e + lo_e + nf_e): slots are assigned in
global token order, so lo_e = clamp(#routed-before-own-range, 0, 1024) and
own kept tokens are the first nf_e own-routed tokens.

Pipeline: routing mask -> capacity scan -> per-expert compaction of own
kept token ids (sparse_gather) -> per-expert dma_gather of the contiguous
row range + dma_scatter_add into the f32 output (base = x + h written
first). Scatters are serialized (cross-call duplicate rows accumulate via
completion-sem chaining; within one call targets are distinct tokens).
"""
import concourse.bass as bass
import concourse.mybir as mybir
from concourse import library_config as libcfg

F32 = mybir.dt.float32
BF16 = mybir.dt.bfloat16
I32 = mybir.dt.int32
I16 = mybir.dt.int16
U32 = mybir.dt.uint32
ALU = mybir.AluOpType
AX = mybir.AxisListType.X


def build_k3(nc: bass.Bass, E):
    DBG = 3
    ytab = nc.dram_tensor("ytab", [32768, 2048], F32, kind="ExternalInput").ap()
    rout = nc.dram_tensor("rout", [16384, 32], F32, kind="ExternalInput").ap()
    rown = nc.dram_tensor("rown", [2048, 32], F32, kind="ExternalInput").ap()
    x_own = nc.dram_tensor("x_own", [2048, 2048], F32, kind="ExternalInput").ap()
    h_own = nc.dram_tensor("h_own", [2048, 2048], BF16, kind="ExternalInput").ap()
    mpre = nc.dram_tensor("mpre", [128, 128], BF16, kind="ExternalInput").ap()
    triu = nc.dram_tensor("triu", [128, 128], BF16, kind="ExternalInput").ap()
    tof3 = nc.dram_tensor("tof3", [128, 512], F32, kind="ExternalInput").ap()
    out_own = nc.dram_tensor("out_own", [2049, 2048], F32,
                             kind="ExternalOutput").ap()
    d_nfc = nc.dram_tensor("d_nfc", [1, 64], U32, kind="ExternalOutput").ap()
    d_si = nc.dram_tensor("d_si", [16, 2048], I16, kind="ExternalOutput").ap()
    d_gi = nc.dram_tensor("d_gi", [16, 2048], I16, kind="ExternalOutput").ap()
    d_sgo = nc.dram_tensor("d_sgo", [16, 2048], F32, kind="ExternalOutput").ap()
    cid_d = nc.dram_tensor("cid_d", [32, 2048], F32, kind="Internal").ap()
    cnt_d = nc.dram_tensor("cnt_d", [16, 32], F32, kind="Internal").ap()
    sidx_d = nc.dram_tensor("sidx_d", [16, 2048], I16, kind="Internal").ap()
    gidx_d = nc.dram_tensor("gidx_d", [16, 2048], I16, kind="Internal").ap()

    s_mpre = E(nc.sbuf_tensor("s_mpre", [128, 128], BF16))
    s_tri = E(nc.sbuf_tensor("s_tri", [128, 128], BF16))
    s_tof = E(nc.sbuf_tensor("s_tof", [128, 32, 16], F32))
    rfull = E(nc.sbuf_tensor("rfull", [128, 128, 32], F32))
    a16 = E(nc.sbuf_tensor("a16", [128, 128, 32], BF16))
    rS = E(nc.sbuf_tensor("rS", [128, 16, 32], F32))
    aown = E(nc.sbuf_tensor("aown", [128, 32, 16], F32))
    rsum = E(nc.sbuf_tensor("rsum", [128, 32], F32))
    rsum16 = E(nc.sbuf_tensor("rsum16", [128, 32], BF16))
    asum = E(nc.sbuf_tensor("asum", [128, 32], F32))
    asum16 = E(nc.sbuf_tensor("asum16", [128, 32], BF16))
    prefx = E(nc.sbuf_tensor("prefx", [128, 32], F32))
    basef = E(nc.sbuf_tensor("basef", [128, 32], F32))
    baseg = E(nc.sbuf_tensor("baseg", [128, 32], F32))
    poso = E(nc.sbuf_tensor("poso", [128, 32, 16], F32))
    kepto = E(nc.sbuf_tensor("kepto", [128, 32, 16], F32))
    cidk = E(nc.sbuf_tensor("cidk", [128, 32, 16], F32))
    zer16 = E(nc.sbuf_tensor("zer16", [128, 16], F32))
    slab = [E(nc.sbuf_tensor(f"slab{i}", [16, 128], F32)) for i in range(2)]
    sgo_all = E(nc.sbuf_tensor("sgo_all", [16, 32, 64], F32))
    sgo = [E(nc.sbuf_tensor(f"sgo{i}", [16, 64], F32)) for i in range(2)]
    nfu = [E(nc.sbuf_tensor(f"nfu{i}", [1, 1], U32)) for i in range(2)]
    io64i = E(nc.sbuf_tensor("io64i", [16, 64], I32))
    io64 = E(nc.sbuf_tensor("io64", [16, 64], F32))
    nf_f = E(nc.sbuf_tensor("nf_f", [1, 32], F32))
    nfc_f = E(nc.sbuf_tensor("nfc_f", [1, 64], F32))
    nfc = E(nc.sbuf_tensor("nfc", [1, 64], U32))
    cnt16 = E(nc.sbuf_tensor("cnt16", [16, 32], F32))
    msk = E(nc.sbuf_tensor("msk", [16, 64], F32))
    tmp1 = E(nc.sbuf_tensor("tmp1", [16, 64], F32))
    sgi32 = E(nc.sbuf_tensor("sgi32", [16, 64], I32))
    sgc = E(nc.sbuf_tensor("sgc", [16, 64], F32))
    scr = E(nc.sbuf_tensor("scr", [1, 8], F32))
    sidx_f = E(nc.sbuf_tensor("sidx_f", [16, 32, 64], F32))
    gidx_f = E(nc.sbuf_tensor("gidx_f", [16, 32, 64], F32))
    si16 = E(nc.sbuf_tensor("si16", [16, 32, 64], I16))
    gi16 = E(nc.sbuf_tensor("gi16", [16, 32, 64], I16))
    si_all = E(nc.sbuf_tensor("si_all", [128, 32, 64], I16))
    gi_all = E(nc.sbuf_tensor("gi_all", [128, 32, 64], I16))
    gbuf = [E(nc.sbuf_tensor(f"gbuf{i}", [128, 4, 2048], F32)) for i in range(2)]
    xt = [E(nc.sbuf_tensor(f"xt{i}", [128, 2048], F32)) for i in range(2)]
    ht = [E(nc.sbuf_tensor(f"ht{i}", [128, 2048], BF16)) for i in range(2)]
    ot = [E(nc.sbuf_tensor(f"ot{i}", [128, 2048], F32)) for i in range(2)]

    pb = E(nc.psum_tensor("pb", [128, 32], F32))
    pb2 = E(nc.psum_tensor("pb2", [128, 32], F32))

    sem_names = ["sLD", "sAS", "sRM16", "sPB", "sBB", "sIO",
                 "sCK", "sCW", "sSL0", "sSL1", "sSG", "sGP", "sGC", "sNF", "sNC", "sCS",
                 "sCL", "sIC", "sIW", "sIR", "sXH0", "sXH1", "sOT", "sST0",
                 "sST1", "sGA", "sSC", "sDBG", "sSV"]
    S = {n: E(nc.semaphore(n)) for n in sem_names}
    block = E(nc.Block())

    obase = out_own[0:2048, :].rearrange("(p f) d -> p f d", f=16)

    @block.sync
    def _(sync):
        for (d, s) in [(s_mpre, mpre), (s_tri, triu)]:
            sync.dma_start(out=d[:], in_=s).then_inc(S["sLD"], 16)
        sync.dma_start(out=s_tof[:],
                       in_=tof3.rearrange("p (e f) -> p e f", f=16)
                       ).then_inc(S["sLD"], 16)
        sync.dma_start(out=rfull[:],
                       in_=rout.rearrange("(p q) e -> p q e", p=128)
                       ).then_inc(S["sLD"], 16)
        sync.dma_start(out=rS[:],
                       in_=rown.rearrange("(p f) e -> p f e", p=128)
                       ).then_inc(S["sLD"], 16)
        if DBG < 2:
            return
        sync.wait_ge(S["sCK"], 1)
        sync.dma_start(out=cid_d.rearrange("e (p f) -> p e f", p=128),
                       in_=cidk[:]).then_inc(S["sCW"], 16)
        sync.wait_ge(S["sCW"], 16)
        for e in range(32):
            if e >= 2:
                sync.wait_ge(S["sSG"], e - 1)
            sync.dma_start(out=slab[e % 2][:],
                           in_=cid_d[e].rearrange("(f p) -> p f", p=16),
                           allow_slow_non_contiguous=True
                           ).then_inc(S[f"sSL{e % 2}"], 16)
        sync.wait_ge(S["sNF"], 1)
        for r in range(16):
            sync.dma_start(out=cnt_d[r:r + 1, :], in_=nf_f[:]
                           ).then_inc(S["sCS"], 16)
        sync.wait_ge(S["sCS"], 256)
        sync.dma_start(out=cnt16[:], in_=cnt_d).then_inc(S["sCL"], 16)
        sync.wait_ge(S["sIC"], 1)
        sync.dma_start(out=sidx_d.rearrange("p (e f) -> p e f", f=64),
                       in_=si16[:]).then_inc(S["sIW"], 16)
        sync.dma_start(out=gidx_d.rearrange("p (e f) -> p e f", f=64),
                       in_=gi16[:]).then_inc(S["sIW"], 16)
        sync.wait_ge(S["sIW"], 32)
        for g8 in range(8):
            sync.dma_start(out=si_all[16 * g8:16 * (g8 + 1), :, :],
                           in_=sidx_d.rearrange("p (e f) -> p e f", f=64)
                           ).then_inc(S["sIR"], 16)
            sync.dma_start(out=gi_all[16 * g8:16 * (g8 + 1), :, :],
                           in_=gidx_d.rearrange("p (e f) -> p e f", f=64)
                           ).then_inc(S["sIR"], 16)
        sync.dma_start(out=d_nfc, in_=nfc[:]).then_inc(S["sDBG"], 16)
        sync.dma_start(out=d_si.rearrange("p (e f) -> p e f", f=64),
                       in_=si16[:]).then_inc(S["sDBG"], 16)
        sync.dma_start(out=d_gi.rearrange("p (e f) -> p e f", f=64),
                       in_=gi16[:]).then_inc(S["sDBG"], 16)
        sync.dma_start(out=d_sgo.rearrange("p (e f) -> p e f", f=64),
                       in_=sgo_all[:]).then_inc(S["sDBG"], 16)
        sync.wait_ge(S["sDBG"], 64)

    @block.scalar
    def _(scalar):
        for q in range(16):
            if q >= 2:
                scalar.wait_ge(S["sOT"], q - 1)
            scalar.dma_start(out=xt[q % 2][:],
                             in_=x_own.rearrange("(p f) d -> p f d", f=16)[:, q, :]
                             ).then_inc(S[f"sXH{q % 2}"], 16)
            scalar.dma_start(out=ht[q % 2][:],
                             in_=h_own.rearrange("(p f) d -> p f d", f=16)[:, q, :]
                             ).then_inc(S[f"sXH{q % 2}"], 16)
            if q >= 1:
                scalar.wait_ge(S["sOT"], q)
                scalar.dma_start(out=obase[:, q - 1, :], in_=ot[(q - 1) % 2][:]
                                 ).then_inc(S[f"sST{(q - 1) % 2}"], 16)
        scalar.wait_ge(S["sOT"], 16)
        scalar.dma_start(out=obase[:, 15, :], in_=ot[1][:]
                         ).then_inc(S["sST1"], 16)

    @block.tensor
    def _(tensor):
        tensor.wait_ge(S["sLD"], 80)
        tensor.wait_ge(S["sRM16"], 1)
        tensor.matmul(pb[:], s_tri[:], rsum16[:], start=True, stop=True
                      ).then_inc(S["sPB"], 1)
        tensor.wait_ge(S["sAS"], 1)
        tensor.matmul(pb2[:], s_mpre[:], asum16[:], start=True, stop=True
                      ).then_inc(S["sBB"], 1)

    @block.vector
    def _(vector):
        cV = [0]
        def vinc(ins):
            cV[0] += 1
            ins.then_inc(S["sSV"], 1)
        def vwait():
            vector.wait_ge(S["sSV"], cV[0])
        vector.wait_ge(S["sLD"], 80)
        vinc(vector.memset(zer16[:], 0.0))
        vinc(vector.tensor_single_scalar(a16[:], rfull[:], 0.0, ALU.is_gt))
        vwait()
        vinc(vector.tensor_reduce(asum[:], a16[:].rearrange("p q e -> p e q"),
                                  axis=AX, op=ALU.add))
        vwait()
        vector.tensor_copy(asum16[:], asum[:]).then_inc(S["sAS"], 1)
        vinc(vector.tensor_single_scalar(aown[:], rS[:].rearrange("p f e -> p e f"),
                                         0.0, ALU.is_gt))
        vwait()
        vinc(vector.tensor_reduce(rsum[:], aown[:], axis=AX, op=ALU.add))
        vwait()
        vector.tensor_copy(rsum16[:], rsum[:]).then_inc(S["sRM16"], 1)
        if DBG >= 2:
            vector.wait_ge(S["sIO"], 1)
            vinc(vector.tensor_copy(io64[:], io64i[:]))
        vector.wait_ge(S["sPB"], 1)
        vinc(vector.tensor_copy(prefx[:], pb[:]))
        vwait()
        vector.wait_ge(S["sBB"], 1)
        vinc(vector.tensor_add(basef[:], prefx[:], pb2[:]))
        vinc(vector.tensor_single_scalar(baseg[:], pb2[:], 1024.0, ALU.min))
        vwait()
        for e in range(32):
            vinc(vector.tensor_tensor_scan(poso[:, e, :], aown[:, e, :],
                                           zer16[:], basef[:, e:e + 1],
                                           ALU.add, ALU.add))
        vwait()
        vinc(vector.tensor_sub(poso[:], poso[:], aown[:]))
        vwait()
        vinc(vector.tensor_single_scalar(kepto[:], poso[:], 1024.0, ALU.is_lt))
        vwait()
        vinc(vector.tensor_mul(kepto[:], kepto[:], aown[:]))
        vwait()
        vinc(vector.tensor_mul(cidk[:], kepto[:], s_tof[:]))
        vwait()
        vinc(vector.tensor_scalar_add(cidk[:], cidk[:], -1.0))
        vwait()
        vector.engine_nop().then_inc(S["sCK"], 1)
        for q in range(16):
            vector.wait_ge(S[f"sXH{q % 2}"], 32 * (q // 2 + 1))
            if q >= 2:
                vector.wait_ge(S[f"sST{q % 2}"], 16 * (q // 2))
            vector.tensor_add(ot[q % 2][:], xt[q % 2][:], ht[q % 2][:]
                              ).then_inc(S["sOT"], 1)
        if DBG < 2:
            return
        for e in range(32):
            vector.wait_ge(S["sSG"], e + 1)
            vinc(vector.tensor_copy(sgo_all[:, e, :], sgo[e % 2][:]))
            vinc(vector.tensor_copy(nf_f[0:1, e:e + 1], nfu[e % 2][:]))
            vwait()
            vector.engine_nop().then_inc(S["sGC"], 1)
        vinc(vector.tensor_single_scalar(nfc_f[0:1, 0:32], nf_f[:],
                                         512.0, ALU.min))
        vinc(vector.tensor_single_scalar(nfc_f[0:1, 32:64], nf_f[:],
                                         512.0, ALU.subtract))
        vwait()
        vinc(vector.tensor_single_scalar(nfc_f[0:1, 32:64], nfc_f[0:1, 32:64],
                                         512.0, ALU.min))
        vwait()
        vinc(vector.tensor_single_scalar(nfc_f[:], nfc_f[:], 1.0, ALU.max))
        vwait()
        vinc(vector.tensor_copy(nfc[:], nfc_f[:]))
        vwait()
        vector.engine_nop().then_inc(S["sNC"], 1)
        vector.engine_nop().then_inc(S["sNF"], 1)
        vector.wait_ge(S["sCL"], 16)
        for e in range(32):
            vinc(vector.tensor_single_scalar(msk[:], io64[:],
                                             cnt16[:, e:e + 1], ALU.is_lt))
            vwait()
            vinc(vector.tensor_scalar_add(tmp1[:], io64[:],
                                          float(1024 * e + 1)))
            vwait()
            vinc(vector.tensor_single_scalar(tmp1[:], tmp1[:],
                                             baseg[0:16, e:e + 1], ALU.add))
            vwait()
            vinc(vector.tensor_mul(tmp1[:], tmp1[:], msk[:]))
            vwait()
            vinc(vector.tensor_scalar_add(gidx_f[:, e, :], tmp1[:], -1.0))
            vwait()
            vinc(vector.tensor_copy(sgi32[:], sgo_all[:, e, :]))
            vwait()
            vinc(vector.tensor_copy(sgc[:], sgi32[:]))
            vwait()
            vinc(vector.tensor_scalar_add(tmp1[:], sgc[:], 1.0))
            vwait()
            vinc(vector.tensor_mul(tmp1[:], tmp1[:], msk[:]))
            vwait()
            vinc(vector.tensor_scalar_add(sidx_f[:, e, :], tmp1[:], -1.0))
            vwait()
            vinc(vector.tensor_single_scalar(scr[0:1, 0:1], nf_f[0:1, e:e + 1],
                                             0.5, ALU.is_lt))
            vinc(vector.tensor_single_scalar(scr[0:1, 1:2], nf_f[0:1, e:e + 1],
                                             512.5, ALU.is_lt))
            vwait()
            vinc(vector.tensor_scalar_mul(scr[0:1, 2:3], scr[0:1, 0:1], 2049.0))
            vinc(vector.tensor_scalar_mul(scr[0:1, 3:4], scr[0:1, 1:2], 2049.0))
            vwait()
            vinc(vector.tensor_single_scalar(sidx_f[0:1, e, 0:1],
                                             sidx_f[0:1, e, 0:1],
                                             scr[0:1, 2:3], ALU.add))
            vinc(vector.tensor_single_scalar(gidx_f[0:1, e, 0:1],
                                             gidx_f[0:1, e, 0:1],
                                             scr[0:1, 0:1], ALU.add))
            vinc(vector.tensor_single_scalar(sidx_f[0:1, e, 32:33],
                                             sidx_f[0:1, e, 32:33],
                                             scr[0:1, 3:4], ALU.add))
            vinc(vector.tensor_single_scalar(gidx_f[0:1, e, 32:33],
                                             gidx_f[0:1, e, 32:33],
                                             scr[0:1, 1:2], ALU.add))
            vwait()
        vinc(vector.tensor_copy(si16[:], sidx_f[:]))
        vinc(vector.tensor_copy(gi16[:], gidx_f[:]))
        vwait()
        vector.engine_nop().then_inc(S["sIC"], 1)

    @block.gpsimd
    def _(g):
        if DBG < 2:
            return
        g.iota(io64i[:], pattern=[[16, 64]], base=0, channel_multiplier=1
               ).then_inc(S["sIO"], 1)
        g.load_library(libcfg.sparse_gather)
        for e in range(32):
            g.wait_ge(S[f"sSL{e % 2}"], 16 * (e // 2 + 1))
            if e >= 2:
                g.wait_ge(S["sGC"], e - 1)
            g.sparse_gather(sgo[e % 2][:], slab[e % 2][:],
                            num_found=nfu[e % 2][:]).then_inc(S["sGP"], 1)
            g.wait_ge(S["sGP"], e + 1)
            g.engine_nop().then_inc(S["sSG"], 1)
        if DBG < 3:
            return
        g.load_library(libcfg.mlp)
        g.wait_ge(S["sNC"], 1)
        g.wait_ge(S["sIR"], 256)
        cr = g.alloc_register()
        for j in range(64):
            e, hh = j // 2, j % 2
            g.reg_load(cr, nfc[0:1, 32 * hh + e:32 * hh + e + 1])
            if j >= 2 and DBG != 4:
                g.wait_ge(S["sSC"], 16 * (j - 1))
            g.dma_gather(gbuf[j % 2][:], ytab, gi_all[:, e, 32 * hh:32 * hh + 32],
                         512, cr, 2048).then_inc(S["sGA"], 16)
            g.wait_ge(S["sGA"], 16 * (j + 1))
            if DBG == 4:
                continue
            if j == 0:
                g.wait_ge(S["sST0"], 16 * 8)
                g.wait_ge(S["sST1"], 16 * 8)
            else:
                g.wait_ge(S["sSC"], 16 * j)
            g.dma_scatter_add(out_own, gbuf[j % 2][:],
                              si_all[:, e, 32 * hh:32 * hh + 32],
                              512, cr, 2048, elem_step=2048
                              ).then_inc(S["sSC"], 16)
        if DBG != 4:
            g.wait_ge(S["sSC"], 16 * 64)
    return nc


def make_k3():
    from contextlib import ExitStack
    from concourse.library_overlay import lower_extended_insts
    nc = bass.Bass("TRN2", target_bir_lowering=False)
    with ExitStack() as ctx:
        build_k3(nc, ctx.enter_context)
    lower_extended_insts(nc)
    return nc


def k3_inputs(ytab_full, rout_full, x_flat, h_full, core):
    import numpy as np
    import ml_dtypes
    BF = ml_dtypes.bfloat16
    mpre = np.tile((np.arange(128) < 16 * core).astype(np.float32)[:, None],
                   (1, 128)).astype(BF)
    tof3 = np.tile((16.0 * np.arange(128, dtype=np.float32)[:, None]
                    + np.arange(16, dtype=np.float32)[None, :] + 1.0), (1, 32))
    return dict(
        ytab=np.ascontiguousarray(ytab_full.astype(np.float32)),
        rout=np.ascontiguousarray(rout_full.astype(np.float32)),
        rown=np.ascontiguousarray(
            rout_full[2048 * core:2048 * (core + 1)].astype(np.float32)),
        x_own=np.ascontiguousarray(
            x_flat[2048 * core:2048 * (core + 1)].astype(np.float32)),
        h_own=np.ascontiguousarray(h_full[2048 * core:2048 * (core + 1)]),
        mpre=mpre,
        triu=np.triu(np.ones((128, 128), np.float32), 1).astype(BF),
        tof3=np.ascontiguousarray(tof3),
    )



def kernel(L0_attn_x, L0_attn_W_in, L0_attn_a_param, L0_ffn_w_cls,
           L0_ffn_bias, L0_ffn_w_gate_up, L0_ffn_w_down):
    import numpy as np
    x = np.ascontiguousarray(np.asarray(L0_attn_x, dtype=np.float32))
    W_in = np.asarray(L0_attn_W_in, dtype=np.float32)
    a_param = np.asarray(L0_attn_a_param, dtype=np.float32)
    w_cls = np.asarray(L0_ffn_w_cls, dtype=np.float32)
    bias = np.asarray(L0_ffn_bias, dtype=np.float32)
    w_gu = np.asarray(L0_ffn_w_gate_up, dtype=np.float32)
    w_dn = np.asarray(L0_ffn_w_down, dtype=np.float32)
    from concourse.bass_utils import run_bass_kernel_spmd
    cores = list(range(8))

    nc1 = make_k1()
    res1 = run_bass_kernel_spmd(
        nc1, [k1_inputs(x, W_in, w_cls, bias, a_param, c) for c in cores],
        core_ids=cores)
    wrf_full = np.concatenate(
        [np.asarray(res1.results[2 * b + H]["wr_out"])[2048 * H:2048 * (H + 1)]
         for b in range(4) for H in range(2)], axis=0)
    h_full = np.concatenate(
        [np.concatenate([np.asarray(res1.results[2 * b]["h_out"]).T,
                         np.asarray(res1.results[2 * b + 1]["h_out"]).T], axis=1)
         for b in range(4)], axis=0)

    x_flat = x.reshape(16384, 2048)
    nc2 = make_k2()
    res2 = run_bass_kernel_spmd(
        nc2, [k2_inputs(x_flat, wrf_full, w_gu, w_dn, c) for c in cores],
        core_ids=cores)
    ytab_full = np.concatenate(
        [np.asarray(res2.results[c]["ytab"]) for c in cores], axis=0)

    nc3 = make_k3()
    res3 = run_bass_kernel_spmd(
        nc3, [k3_inputs(ytab_full, wrf_full, x_flat, h_full, c) for c in cores],
        core_ids=cores)
    out = np.concatenate(
        [np.asarray(res3.results[c]["out_own"], np.float32)[0:2048]
         for c in cores], axis=0)
    return np.ascontiguousarray(out.reshape(4, 4096, 2048))


# version 02 (final)
# speedup vs baseline: 1.0166x; 1.0166x over previous
import concourse.bass as bass
import concourse.mybir as mybir

F32 = mybir.dt.float32
BF16 = mybir.dt.bfloat16
AF = mybir.ActivationFunctionType
ALU = mybir.AluOpType
AX = mybir.AxisListType.X


def build_k1(nc: bass.Bass, E):
    xb = nc.dram_tensor("xroll", [4096, 2048], F32, kind="ExternalInput").ap()
    winT = nc.dram_tensor("winT", [2048, 2048], F32, kind="ExternalInput").ap()
    wclsT = nc.dram_tensor("wclsT", [2048, 64], F32, kind="ExternalInput").ap()
    biasb = nc.dram_tensor("biasb", [128, 64], F32, kind="ExternalInput").ap()
    apar = nc.dram_tensor("apar", [128, 1], F32, kind="ExternalInput").ap()
    ident = nc.dram_tensor("ident", [128, 128], BF16, kind="ExternalInput").ap()

    h_out = nc.dram_tensor("h_out", [1024, 4096], BF16, kind="ExternalOutput").ap()
    wr_out = nc.dram_tensor("wr_out", [4096, 32], F32, kind="ExternalOutput").ap()
    f_out = nc.dram_tensor("f_out", [4096, 1], F32, kind="ExternalOutput").ap()

    a_dram = nc.dram_tensor("a_spill", [1024, 4096], F32, kind="Internal").ap()
    w_dram = nc.dram_tensor("w_spill", [1024, 4096], BF16, kind="Internal").ap()
    z_dram = nc.dram_tensor("z_spill", [1024, 4096], BF16, kind="Internal").ap()

    xn_fm = E(nc.sbuf_tensor("xn_fm", [128, 16, 2048], BF16))
    s_id = E(nc.sbuf_tensor("s_id", [128, 128], BF16))
    s_bias = E(nc.sbuf_tensor("s_bias", [128, 64], F32))
    s_apar = E(nc.sbuf_tensor("s_apar", [128, 1], F32))
    m8af = E(nc.sbuf_tensor("m8af", [128, 1], F32))
    sp_t = E(nc.sbuf_tensor("sp_t", [128, 2], F32))
    stc = [E(nc.sbuf_tensor(f"stc{k}", [128, 8], F32)) for k in range(2)]
    wc32 = E(nc.sbuf_tensor("wc32", [128, 16, 64], F32))
    wcb = E(nc.sbuf_tensor("wcb", [128, 16, 64], BF16))
    e_sb = [E(nc.sbuf_tensor(f"e_sb{k}", [128, 64], F32)) for k in range(2)]
    rc = [E(nc.sbuf_tensor(f"rc{k}", [128, 8], F32)) for k in range(2)]
    r_sc = E(nc.sbuf_tensor("r_sc", [128, 64], F32))
    r_sp = E(nc.sbuf_tensor("r_sp", [128, 64], F32))
    r_sel = E(nc.sbuf_tensor("r_sel", [128, 64], F32))
    r_m8 = E(nc.sbuf_tensor("r_m8", [128, 8], F32))
    wr_sb = [E(nc.sbuf_tensor(f"wr_sb{k}", [128, 64], F32)) for k in range(2)]
    f_sb = [E(nc.sbuf_tensor(f"f_sb{k}", [128, 1], F32)) for k in range(2)]
    arena = E(nc.sbuf_tensor("arena", [128, 28672], F32))
    ar32 = arena
    ar16 = arena.bitcast(BF16)
    def cv32(byte_off, cols, shape=None):
        ap = ar32[:, byte_off // 4: byte_off // 4 + cols]
        return ap if shape is None else ap.rearrange(
            "p (" + " ".join(chr(97 + i) for i in range(len(shape))) + ") -> p "
            + " ".join(chr(97 + i) for i in range(len(shape))),
            **{chr(97 + i): s for i, s in enumerate(shape)})
    def cv16(byte_off, cols, shape=None):
        ap = ar16[:, byte_off // 2: byte_off // 2 + cols]
        return ap if shape is None else ap.rearrange(
            "p (" + " ".join(chr(97 + i) for i in range(len(shape))) + ") -> p "
            + " ".join(chr(97 + i) for i in range(len(shape))),
            **{chr(97 + i): s for i, s in enumerate(shape)})
    xt = [cv32(0, 2048), cv32(8192, 2048)]
    sq = cv32(16384, 2048)
    xn_tm = [cv16(24576, 2048), cv16(28672, 2048)]
    w32 = [cv32(32768, 4096, (2, 16, 128)), cv32(49152, 4096, (2, 16, 128))]
    wb = [cv16(65536, 4096, (2, 16, 128)), cv16(73728, 4096, (2, 16, 128))]
    gxs = [cv32(81920, 512), cv32(83968, 512)]
    gas = [cv32(86016, 512), cv32(88064, 512)]
    t1 = cv32(90112, 512)
    gs2 = [cv32(92160, 512), cv32(94208, 512)]
    a_sb = [cv32(96256, 512), cv32(98304, 512)]
    u_sb = [cv32(100352, 512), cv32(102400, 512)]
    w_sb = [cv16(104448, 512), cv16(105472, 512)]
    z_sb = [cv16(106496, 512), cv16(107520, 512)]
    a_ld = [cv32(0, 4096), cv32(16384, 4096)]
    z_ld = [cv16(32768, 4096), cv16(40960, 4096)]
    wl_ld = [cv16(49152, 4096), cv16(57344, 4096)]
    mu_sb = [cv16(65536, 4096), cv16(73728, 4096)]
    b_sb = cv32(81920, 4096)
    hb_sb = [cv16(98304, 4096), cv16(106496, 4096)]

    tr = [E(nc.psum_tensor(f"tr{k}", [128, 1024], BF16)) for k in range(2)]
    pgx = [E(nc.psum_tensor(f"pgx{k}", [128, 512], F32)) for k in range(2)]
    pga = [E(nc.psum_tensor(f"pga{k}", [128, 512], F32)) for k in range(2)]
    pr = [E(nc.psum_tensor(f"pr{k}", [128, 64], F32)) for k in range(2)]

    sem_names = ["sLC", "sLX0", "sLX1", "sSQ", "sVS", "sRS", "sNB", "sXN", "sTP",
                 "sTE", "sLW0", "sLW1", "sWC", "sPE", "sTH", "sGS",
                 "sA", "sW", "sZ", "sSPA0", "sSPA1", "sSPW0", "sSPW1",
                 "sSPZ0", "sSPZ1", "sLR", "sWCR",
                 "sRT", "sRM", "sRE", "sWR", "sSWR0", "sSWR1", "sM8",
                 "sL20", "sL21", "sMU", "sHB", "sSH0", "sSH1", "sSC", "sSS", "sSV"]
    S = {n: E(nc.semaphore(n)) for n in sem_names}
    block = E(nc.Block())

    @block.sync
    def _(sync):
        sync.dma_start(out=s_id[:], in_=ident).then_inc(S["sLC"], 16)
        sync.dma_start(out=s_bias[:], in_=biasb).then_inc(S["sLC"], 16)
        sync.dma_start(out=s_apar[:], in_=apar).then_inc(S["sLC"], 16)
        for blk in range(2):
            for t in range(16):
                g = blk * 16 + t
                if g >= 2:
                    sync.wait_ge(S["sXN"], g - 1)
                sync.dma_start(out=xt[g % 2][:],
                               in_=xb[128 * g:128 * (g + 1), :]).then_inc(S[f"sLX{g % 2}"], 16)
            for i in range(8):
                P = blk * 8 + i
                if P >= 2:
                    sync.wait_ge(S["sWC"], P - 1)
                sync.dma_start(
                    out=w32[P % 2][:, 0],
                    in_=winT[:, 128 * i:128 * (i + 1)]
                    .rearrange("(hc p) g -> p hc g", p=128)).then_inc(S[f"sLW{P % 2}"], 16)
                sync.dma_start(
                    out=w32[P % 2][:, 1],
                    in_=winT[:, 1024 + 128 * i:1024 + 128 * (i + 1)]
                    .rearrange("(hc p) g -> p hc g", p=128)).then_inc(S[f"sLW{P % 2}"], 16)
            if blk == 0:
                sync.dma_start(
                    out=wc32[:],
                    in_=wclsT.rearrange("(hc p) g -> p hc g", p=128)
                ).then_inc(S["sLR"], 16)
        for q in range(2):
            sync.wait_ge(S[f"sSPA{q}"], 16 * 32)
            sync.wait_ge(S[f"sSPW{q}"], 16 * 32)
            sync.wait_ge(S[f"sSPZ{q}"], 16 * 32)
        for i in range(8):
            if i >= 2:
                sync.wait_ge(S["sSC"], i - 1)
            sync.dma_start(out=a_ld[i % 2][:],
                           in_=a_dram[128 * i:128 * (i + 1), :]).then_inc(S[f"sL2{i % 2}"], 16)
            sync.dma_start(out=z_ld[i % 2][:],
                           in_=z_dram[128 * i:128 * (i + 1), :]).then_inc(S[f"sL2{i % 2}"], 16)
            sync.dma_start(out=wl_ld[i % 2][:],
                           in_=w_dram[128 * i:128 * (i + 1), :]).then_inc(S[f"sL2{i % 2}"], 16)

    @block.scalar
    def _(scalar):
        cS = [0]
        def sinc(ins):
            cS[0] += 1
            ins.then_inc(S["sSS"], 1)
        def swait():
            scalar.wait_ge(S["sSS"], cS[0])
        scalar.wait_ge(S["sLC"], 48)
        sinc(scalar.activation(sp_t[:, 0:1], s_apar[:], AF.Exp))
        swait()
        sinc(scalar.add(sp_t[:, 1:2], sp_t[:, 0:1], 1.0))
        swait()
        sinc(scalar.activation(sp_t[:, 0:1], sp_t[:, 1:2], AF.Ln))
        swait()
        scalar.mul(m8af[:], sp_t[:, 0:1], -8.0).then_inc(S["sM8"], 1)
        for blk in range(2):
            for t in range(16):
                g = blk * 16 + t
                p = g % 2
                scalar.wait_ge(S[f"sLX{g % 2}"], 16 * (g // 2 + 1))
                if g >= 2:
                    scalar.wait_ge(S["sVS"], g - 1)
                scalar.activation(sq[:], xt[p][:], AF.Square,
                                  accum_out=stc[p][:, 0:1]).then_inc(S["sSQ"], 1)
                scalar.wait_ge(S["sVS"], g + 1)
                sinc(scalar.activation(stc[p][:, 4:5], stc[p][:, 3:4], AF.Ln))
                swait()
                scalar.activation(stc[p][:, 5:6], stc[p][:, 4:5], AF.Exp,
                                  scale=-0.5).then_inc(S["sRS"], 1)
                scalar.wait_ge(S["sNB"], g + 1)
                if g >= 2:
                    scalar.wait_ge(S["sTP"], 2 * g - 2)
                scalar.activation(xn_tm[p][:], xt[p][:], AF.Identity,
                                  bias=stc[p][:, 6:7], scale=stc[p][:, 5:6]
                                  ).then_inc(S["sXN"], 1)
            for i in range(8):
                P = blk * 8 + i
                for tc in range(4):
                    u = 4 * P + tc
                    scalar.wait_ge(S["sPE"], u + 1)
                    if u >= 2:
                        scalar.wait_ge(S["sGS"], u - 1)
                    scalar.activation(gxs[u % 2][:], pgx[u % 2][:], AF.Tanh,
                                      scale=0.5)
                    scalar.activation(gas[u % 2][:], pga[u % 2][:], AF.Tanh,
                                      scale=0.5).then_inc(S["sTH"], 1)
                    scalar.wait_ge(S["sGS"], u + 1)
                    scalar.wait_ge(S["sM8"], 1)
                    if u >= 2:
                        scalar.wait_ge(S[f"sSPA{u % 2}"], 16 * (u // 2))
                    if u >= 2:
                        scalar.wait_ge(S["sZ"], u - 1)
                    sinc(scalar.activation(a_sb[u % 2][:], gs2[u % 2][:], AF.Exp,
                                      scale=m8af[:, 0:1]))
                    swait()
                    scalar.activation(u_sb[u % 2][:], a_sb[u % 2][:], AF.Square
                                      ).then_inc(S["sA"], 1)
            for t in range(16):
                rt = blk * 16 + t
                scalar.wait_ge(S["sRM"], rt + 1)
                if rt >= 2:
                    scalar.wait_ge(S["sWR"], rt - 1)
                scalar.activation(e_sb[rt % 2][:], pr[rt % 2][:], AF.Exp,
                                  bias=rc[rt % 2][:, 0:1]).then_inc(S["sRE"], 1)
        for i in range(8):
            scalar.wait_ge(S[f"sL2{i % 2}"], 48 * (i // 2 + 1))
            if i >= 2:
                scalar.wait_ge(S["sHB"], i - 1)
            scalar.activation(mu_sb[i % 2][:], z_ld[i % 2][:], AF.Sqrt
                              ).then_inc(S["sMU"], 1)

    @block.vector
    def _(vector):
        cV = [0]
        def vinc(ins):
            cV[0] += 1
            ins.then_inc(S["sSV"], 1)
        def vwait():
            vector.wait_ge(S["sSV"], cV[0])
        for blk in range(2):
            for t in range(16):
                g = blk * 16 + t
                p = g % 2
                vector.wait_ge(S[f"sLX{g % 2}"], 16 * (g // 2 + 1))
                if g >= 2:
                    vector.wait_ge(S["sXN"], g - 1)
                vinc(vector.reduce_sum(stc[p][:, 1:2], xt[p][:], axis=AX, negate=True))
                vwait()
                vinc(vector.tensor_scalar_mul(stc[p][:, 1:2], stc[p][:, 1:2], 1.0 / 2048.0))
                vector.wait_ge(S["sSQ"], g + 1)
                vinc(vector.tensor_scalar_mul(stc[p][:, 2:3], stc[p][:, 0:1], 1.0 / 2048.0))
                vwait()
                vinc(vector.tensor_mul(stc[p][:, 3:4], stc[p][:, 1:2], stc[p][:, 1:2]))
                vwait()
                vinc(vector.tensor_sub(stc[p][:, 3:4], stc[p][:, 2:3], stc[p][:, 3:4]))
                vwait()
                vector.tensor_scalar_add(stc[p][:, 3:4], stc[p][:, 3:4], 1e-5
                                         ).then_inc(S["sVS"], 1)
                vector.wait_ge(S["sRS"], g + 1)
                vwait()
                vector.tensor_mul(stc[p][:, 6:7], stc[p][:, 1:2], stc[p][:, 5:6]
                                  ).then_inc(S["sNB"], 1)
                for gg in range(2):
                    tg = 2 * g + gg
                    vector.wait_ge(S["sTP"], tg + 1)
                    vector.tensor_copy(
                        xn_fm[:, 8 * gg:8 * (gg + 1), 128 * t:128 * (t + 1)],
                        tr[tg % 2][:].rearrange("p (j t2) -> p j t2", t2=128),
                    ).then_inc(S["sTE"], 1)
            for i in range(8):
                P = blk * 8 + i
                for tc in range(4):
                    u = 4 * P + tc
                    vector.wait_ge(S["sTH"], u + 1)
                    vinc(vector.tensor_scalar(t1[:], gxs[u % 2][:], 0.5, 0.5,
                                         ALU.mult, ALU.add))
                    if u >= 2:
                        vector.wait_ge(S[f"sSPW{u % 2}"], 16 * (u // 2))
                    vwait()
                    vector.tensor_mul(w_sb[u % 2][:], t1[:],
                                      xn_fm[:, i, 512 * tc:512 * (tc + 1)]
                                      ).then_inc(S["sW"], 1)
                    if u >= 2:
                        vector.wait_ge(S["sA"], u - 1)
                    vector.tensor_scalar(gs2[u % 2][:], gas[u % 2][:], 0.5, 0.5,
                                         ALU.mult, ALU.add).then_inc(S["sGS"], 1)
                    vector.wait_ge(S["sA"], u + 1)
                    if u >= 2:
                        vector.wait_ge(S[f"sSPZ{u % 2}"], 16 * (u // 2))
                    vector.tensor_scalar(z_sb[u % 2][:], u_sb[u % 2][:], -1.0, 1.0,
                                         ALU.mult, ALU.add).then_inc(S["sZ"], 1)
            for t in range(16):
                rt = blk * 16 + t
                q = rt % 2
                vector.wait_ge(S["sRT"], rt + 1)
                if rt >= 2:
                    vector.wait_ge(S["sRE"], rt - 1)
                vector.reduce_max(rc[q][:, 0:1], pr[q][:], axis=AX, negate=True
                                  ).then_inc(S["sRM"], 1)
                vector.wait_ge(S["sRE"], rt + 1)
                vinc(vector.reduce_sum(rc[q][:, 1:2], e_sb[q][:], axis=AX))
                vwait()
                vinc(vector.reciprocal(rc[q][:, 2:3], rc[q][:, 1:2]))
                vwait()
                vinc(vector.tensor_scalar_mul(r_sc[:], e_sb[q][:], rc[q][:, 2:3]))
                vwait()
                vinc(vector.tensor_add(r_sp[:], r_sc[:], s_bias[:]))
                vwait()
                vinc(vector.max(out=r_m8[:], in_=r_sp[:]))
                vwait()
                vinc(vector.tensor_single_scalar(r_sel[:], r_sp[:], r_m8[:, 7:8],
                                            ALU.is_ge))
                vwait()
                vinc(vector.tensor_mul(r_sc[:], r_sc[:], r_sel[:]))
                vwait()
                vinc(vector.reduce_sum(rc[q][:, 3:4], r_sc[:], axis=AX))
                vwait()
                vinc(vector.tensor_scalar_add(rc[q][:, 3:4], rc[q][:, 3:4], 1e-20))
                vwait()
                vinc(vector.reciprocal(rc[q][:, 4:5], rc[q][:, 3:4]))
                vwait()
                vinc(vector.tensor_scalar_mul(rc[q][:, 4:5], rc[q][:, 4:5], 1.5))
                if rt >= 2:
                    vector.wait_ge(S[f"sSWR{rt % 2}"], 32 * (rt // 2))
                vwait()
                vinc(vector.tensor_scalar_mul(wr_sb[q][:], r_sc[:], rc[q][:, 4:5]))
                vwait()
                vinc(vector.reduce_sum(f_sb[q][:], wr_sb[q][:, 32:64], axis=AX))
                vwait()
                vinc(vector.tensor_scalar_add(f_sb[q][:], f_sb[q][:], 1.0))
                vwait()
                vector.tensor_scalar_mul(wr_sb[q][:, 0:32], wr_sb[q][:, 0:32],
                                         f_sb[q][:, 0:1]).then_inc(S["sWR"], 1)
        for i in range(8):
            vector.wait_ge(S["sMU"], i + 1)
            if i >= 1:
                vector.wait_ge(S["sHB"], i)
            vinc(vector.tensor_mul(b_sb[:], mu_sb[i % 2][:], wl_ld[i % 2][:]))
            if i >= 2:
                vector.wait_ge(S[f"sSH{i % 2}"], 16 * (i // 2))
            vwait()
            vector.tensor_tensor_scan(hb_sb[i % 2][:], a_ld[i % 2][:], b_sb[:],
                                      0.0, ALU.mult, ALU.add).then_inc(S["sHB"], 1)
            vector.engine_nop().then_inc(S["sSC"], 1)

    @block.tensor
    def _(tensor):
        tensor.wait_ge(S["sLC"], 48)
        for blk in range(2):
            for t in range(16):
                g = blk * 16 + t
                tensor.wait_ge(S["sXN"], g + 1)
                for gg in range(2):
                    tg = 2 * g + gg
                    if tg >= 2:
                        tensor.wait_ge(S["sTE"], tg - 1)
                    for j in range(8):
                        jj = 8 * gg + j
                        ins = tensor.transpose(
                            tr[tg % 2][:, 128 * j:128 * (j + 1)],
                            xn_tm[g % 2][:, 128 * jj:128 * (jj + 1)],
                            s_id[:])
                    ins.then_inc(S["sTP"], 1)
            tensor.wait_ge(S["sTE"], 32 * (blk + 1))
            for i in range(8):
                P = blk * 8 + i
                tensor.wait_ge(S["sWC"], P + 1)
                for tc in range(4):
                    u = 4 * P + tc
                    if u >= 2:
                        tensor.wait_ge(S["sTH"], u - 1)
                    for half in range(2):
                        dst = (pgx if half == 0 else pga)[u % 2]
                        for hc in range(16):
                            ins = tensor.matmul(
                                dst[:],
                                wb[P % 2][:, half, hc, :],
                                xn_fm[:, hc, 512 * tc:512 * (tc + 1)],
                                start=(hc == 0), stop=(hc == 15))
                        if half == 1:
                            ins.then_inc(S["sPE"], 1)
            tensor.wait_ge(S["sWCR"], 1)
            for t in range(16):
                rt = blk * 16 + t
                if rt >= 2:
                    tensor.wait_ge(S["sRE"], rt - 1)
                for hc in range(16):
                    ins = tensor.matmul(
                        pr[rt % 2][:],
                        xn_fm[:, hc, 128 * t:128 * (t + 1)],
                        wcb[:, hc, :],
                        start=(hc == 0), stop=(hc == 15))
                ins.then_inc(S["sRT"], 1)

    @block.gpsimd
    def _(g):
        def spills(P):
            blkP, iP = P // 8, P % 8
            for tc in range(4):
                u = 4 * P + tc
                cs = slice(2048 * blkP + 512 * tc, 2048 * blkP + 512 * (tc + 1))
                g.wait_ge(S["sA"], u + 1)
                g.dma_start(out=a_dram[128 * iP:128 * (iP + 1), cs],
                            in_=a_sb[u % 2][:]).then_inc(S[f"sSPA{u % 2}"], 16)
                g.wait_ge(S["sW"], u + 1)
                g.dma_start(out=w_dram[128 * iP:128 * (iP + 1), cs],
                            in_=w_sb[u % 2][:]).then_inc(S[f"sSPW{u % 2}"], 16)
                g.wait_ge(S["sZ"], u + 1)
                g.dma_start(out=z_dram[128 * iP:128 * (iP + 1), cs],
                            in_=z_sb[u % 2][:]).then_inc(S[f"sSPZ{u % 2}"], 16)

        def rstores(rts):
            for rt in rts:
                g.wait_ge(S["sWR"], rt + 1)
                g.dma_start(out=wr_out[128 * rt:128 * (rt + 1), :],
                            in_=wr_sb[rt % 2][:, 0:32]).then_inc(S[f"sSWR{rt % 2}"], 16)
                g.dma_start(out=f_out[128 * rt:128 * (rt + 1), :],
                            in_=f_sb[rt % 2][:]).then_inc(S[f"sSWR{rt % 2}"], 16)
        for P in range(16):
            if P >= 1:
                spills(P - 1)
            if P == 8:
                rstores(range(16))
            g.wait_ge(S[f"sLW{P % 2}"], 32 * (P // 2 + 1))
            if P >= 2:
                g.wait_ge(S["sPE"], 4 * (P - 1))
            g.tensor_copy(wb[P % 2][:], w32[P % 2][:]).then_inc(S["sWC"], 1)
            if P == 7:
                g.wait_ge(S["sLR"], 16)
                g.tensor_copy(wcb[:], wc32[:]).then_inc(S["sWCR"], 1)
        spills(15)
        rstores(range(16, 32))
        for i in range(8):
            g.wait_ge(S["sHB"], i + 1)
            g.dma_start(out=h_out[128 * i:128 * (i + 1), :],
                        in_=hb_sb[i % 2][:]).then_inc(S[f"sSH{i % 2}"], 16)
        for q in range(2):
            g.wait_ge(S[f"sSH{q}"], 16 * 4)
            g.wait_ge(S[f"sSWR{q}"], 16 * 32)
    return nc


def make_k1():
    from contextlib import ExitStack
    nc = bass.Bass("TRN2", target_bir_lowering=False)
    with ExitStack() as ctx:
        build_k1(nc, ctx.enter_context)
    return nc


def k1_inputs(x, W_in, w_cls, bias, a_param, core):
    import numpy as np
    import ml_dtypes
    b, H = core // 2, core % 2
    xb = np.asarray(x[b])
    xroll = np.ascontiguousarray(np.roll(xb, -1024 * H, axis=1))
    WT = np.asarray(W_in).T
    WTr = np.roll(WT, -1024 * H, axis=0)
    winT = np.ascontiguousarray(np.concatenate(
        [WTr[:, 1024 * H:1024 * (H + 1)],
         WTr[:, 2048 + 1024 * H:2048 + 1024 * (H + 1)]], axis=1))
    wclsT = np.ascontiguousarray(np.roll(np.asarray(w_cls).T, -1024 * H, axis=0))
    biasb = np.ascontiguousarray(np.tile(np.asarray(bias)[None, :], (128, 1)))
    apar = np.ascontiguousarray(np.asarray(a_param)[:, None])
    ident = np.eye(128, dtype=ml_dtypes.bfloat16)
    return dict(xroll=xroll.astype(np.float32), winT=winT.astype(np.float32),
                wclsT=wclsT.astype(np.float32), biasb=biasb.astype(np.float32),
                apar=apar.astype(np.float32), ident=ident)



"""K2: expert-sharded MoE compute. Core c owns experts 4c..4c+3. See NOTES.md."""
import os
import concourse.bass as bass
import concourse.mybir as mybir
from concourse import library_config as libcfg
K2DBG = False

F32 = mybir.dt.float32
BF16 = mybir.dt.bfloat16
I16 = mybir.dt.int16
I32 = mybir.dt.int32
U32 = mybir.dt.uint32
AF = mybir.ActivationFunctionType
ALU = mybir.AluOpType
AX = mybir.AxisListType.X


def build_k2(nc: bass.Bass, E):
    x_full = nc.dram_tensor("x_full", [16384, 2048], F32, kind="ExternalInput").ap()
    rout = nc.dram_tensor("rout", [16384, 4], F32, kind="ExternalInput").ap()
    wgu = nc.dram_tensor("wgu", [4, 2048, 2048], F32, kind="ExternalInput").ap()
    wdn = nc.dram_tensor("wdn", [4, 1024, 2048], F32, kind="ExternalInput").ap()
    ident = nc.dram_tensor("ident", [128, 128], BF16, kind="ExternalInput").ap()
    triu = nc.dram_tensor("triu", [128, 128], BF16, kind="ExternalInput").ap()
    ytab = nc.dram_tensor("ytab", [4096, 2048], F32, kind="ExternalOutput").ap()
    if K2DBG:
        d_idcq = nc.dram_tensor("d_idcq", [512, 64], I16, kind="ExternalOutput").ap()
        d_nfc = nc.dram_tensor("d_nfc", [1, 8], U32, kind="ExternalOutput").ap()
        d_wr = nc.dram_tensor("d_wr", [512, 8], F32, kind="ExternalOutput").ap()
        d_xg = nc.dram_tensor("d_xg", [512, 2048], F32, kind="ExternalOutput").ap()
        d_buf = nc.dram_tensor("d_buf", [128, 16384], BF16, kind="ExternalOutput").ap()
        d_act = nc.dram_tensor("d_act", [128, 8192], BF16, kind="ExternalOutput").ap()

    cid_d = nc.dram_tensor("cid_d", [4, 16384], F32, kind="Internal").ap()
    wrc_d = nc.dram_tensor("wrc_d", [4, 16384], F32, kind="Internal").ap()
    idq_d = nc.dram_tensor("idq_d", [4, 16, 64], I16, kind="Internal").ap()
    wrq_d = nc.dram_tensor("wrq_d", [4, 1024], F32, kind="Internal").ap()
    cnt_d = nc.dram_tensor("cnt_d", [32], F32, kind="Internal").ap()
    cnt_d2 = nc.dram_tensor("cnt_d2", [16], F32, kind="Internal").ap()

    s_id = E(nc.sbuf_tensor("s_id", [128, 128], BF16))
    s_tri = E(nc.sbuf_tensor("s_tri", [128, 128], BF16))
    idcq = [E(nc.sbuf_tensor(f"idcq{k}", [128, 64], I16)) for k in range(4)]
    idc16 = E(nc.sbuf_tensor("idc16", [16, 64], I16))
    nfe = [E(nc.sbuf_tensor(f"nfe{k}", [1, 1], U32)) for k in range(4)]
    nfx = E(nc.sbuf_tensor("nfx", [1, 1], U32))
    nf_f = E(nc.sbuf_tensor("nf_f", [1, 16], F32))
    nfc = E(nc.sbuf_tensor("nfc", [1, 8], U32))
    wr128 = [E(nc.sbuf_tensor(f"wr128_{k}", [128, 8], F32)) for k in range(4)]
    stc = [E(nc.sbuf_tensor(f"stc{k}", [128, 8], F32)) for k in range(2)]
    idc_f = E(nc.sbuf_tensor("idc_f", [16, 64], F32))
    wrs_f = E(nc.sbuf_tensor("wrs_f", [16, 64], F32))
    slab_c = E(nc.sbuf_tensor("slab_c", [16, 1024], F32))
    slab_w = E(nc.sbuf_tensor("slab_w", [16, 1024], F32))
    mid_c = E(nc.sbuf_tensor("mid_c", [16, 128], F32))
    mid_w = E(nc.sbuf_tensor("mid_w", [16, 128], F32))
    nfa = E(nc.sbuf_tensor("nfa", [1, 1], U32))
    nfb = E(nc.sbuf_tensor("nfb", [1, 1], U32))
    ones16 = E(nc.sbuf_tensor("ones16", [1, 32], F32))
    cnt_row = E(nc.sbuf_tensor("cnt_row", [1, 32], F32))
    cnt16 = E(nc.sbuf_tensor("cnt16", [16, 2], F32))
    cnt16f = E(nc.sbuf_tensor("cnt16f", [16, 1], F32))
    io64i = E(nc.sbuf_tensor("io64i", [16, 64], I32))
    io64 = E(nc.sbuf_tensor("io64", [16, 64], F32))
    mska = E(nc.sbuf_tensor("mska", [16, 64], F32))
    mskb = E(nc.sbuf_tensor("mskb", [16, 64], F32))
    tmsk = E(nc.sbuf_tensor("tmsk", [16, 64], F32))
    tid_i = E(nc.sbuf_tensor("tid_i", [128, 128], I32))
    sq_scr = E(nc.sbuf_tensor("sq_scr", [128, 2048], F32))

    arena = E(nc.sbuf_tensor("arena", [128, 47104], F32))
    ar32 = arena
    ar16 = arena.bitcast(BF16)

    def cv32(off, cols, shape=None):
        ap = ar32[:, off // 4: off // 4 + cols]
        if shape is None:
            return ap
        pat = " ".join(chr(97 + i) for i in range(len(shape)))
        return ap.rearrange(f"p ({pat}) -> p {pat}",
                            **{chr(97 + i): s for i, s in enumerate(shape)})

    def cv16(off, cols, shape=None):
        ap = ar16[:, off // 2: off // 2 + cols]
        if shape is None:
            return ap
        pat = " ".join(chr(97 + i) for i in range(len(shape)))
        return ap.rearrange(f"p ({pat}) -> p {pat}",
                            **{chr(97 + i): s for i, s in enumerate(shape)})

    routS = cv32(0, 512, (128, 4))
    A_fe = cv32(2048, 512, (128, 4))
    posb = cv32(4096, 512, (4, 128))
    keepb = cv32(6144, 512, (4, 128))
    cida = cv32(8192, 512, (4, 128))
    wrca = cv32(10240, 512, (4, 128))
    tid_f = cv32(12288, 128)
    tidp1 = cv32(12800, 128)
    zeros128 = cv32(13312, 128)
    rs32 = cv32(13824, 4)
    rs16 = cv16(13856, 4)
    pref = cv32(13888, 4)
    xg = cv32(0, 8192, (4, 2048))
    xng = [cv16(32768 + 4096 * k, 2048) for k in range(2)]
    buf_fm = cv16(40960, 16384, (16, 1024))
    sg_sb = cv16(73728, 8192, (8, 1024))
    act_sb = cv16(90112, 8192, (8, 1024))
    wgur = cv16(106496, 16384, (16, 1024))
    wstg = [cv32(172032 + 8192 * k, 2048) for k in range(2)]
    wdnr = cv16(139264, 16384, (8, 2048))
    ysb = [cv32(73728 + 8192 * k, 2048) for k in range(2)]

    tr = [E(nc.psum_tensor(f"tr{k}", [128, 1024], BF16)) for k in range(2)]
    pgu = [E(nc.psum_tensor(f"pgu{k}", [128, 512], F32)) for k in range(2)]
    pdn = [E(nc.psum_tensor(f"pdn{k}", [128, 512], F32)) for k in range(2)]
    ppre = E(nc.psum_tensor("ppre", [128, 4], F32))

    sem_names = ["sLD", "sIOTA", "sRS16", "sPRE", "sPREV", "sCID",
                 "sBO0", "sBO1", "sSLC", "sSLW", "sMS", "sSG", "sCV", "sCNT",
                 "sIDW0", "sIDW1", "sIDW2", "sIDW3", "sIDR0", "sIDR1", "sIDR2", "sIDR3", "sWRW0", "sWRW1", "sWRW2", "sWRW3", "sWRL0", "sWRL1", "sWRL2", "sWRL3",
                 "sGA0", "sGA1", "sSQ2", "sVS2", "sRS2", "sNB2", "sXN2",
                 "sTP2", "sTE2",
                 "sWG0", "sWG1", "sWGC", "sWD0", "sWD1", "sWDC",
                 "sGU", "sSIL", "sACT", "sDN", "sYS", "sYW0", "sYW1",
                 "sSS", "sSV", "sXGZ", "sGP", "sS1", "sCR", "sCL", "sMK", "sCR2", "sCL2"]
    S = {n: E(nc.semaphore(n)) for n in sem_names}
    block = E(nc.Block())

    @block.sync
    def _(sync):
        sync.dma_start(out=s_id[:], in_=ident).then_inc(S["sLD"], 16)
        sync.dma_start(out=s_tri[:], in_=triu).then_inc(S["sLD"], 16)
        sync.dma_start(out=routS[:],
                       in_=rout.rearrange("(p f) e -> p f e", p=128)
                       ).then_inc(S["sLD"], 16)
        sync.wait_ge(S["sCID"], 1)
        sync.dma_start(out=cid_d.rearrange("e (p f) -> p e f", p=128),
                       in_=cida[:]).then_inc(S["sBO0"], 16)
        sync.dma_start(out=wrc_d.rearrange("e (p f) -> p e f", p=128),
                       in_=wrca[:]).then_inc(S["sBO1"], 16)
        for e in range(4):
            if e == 0:
                sync.wait_ge(S["sBO0"], 16)
                sync.wait_ge(S["sBO1"], 16)
            else:
                sync.wait_ge(S["sSG"], 2 * e)
            sync.dma_start(out=slab_c[:], in_=cid_d[e].rearrange("(f p) -> p f", p=16),
                           allow_slow_non_contiguous=True).then_inc(S["sSLC"], 16)
            sync.dma_start(out=slab_w[:], in_=wrc_d[e].rearrange("(f p) -> p f", p=16),
                           allow_slow_non_contiguous=True).then_inc(S["sSLW"], 16)
            sync.wait_ge(S["sCR"], e + 1)
            sync.dma_start(out=cnt_d, in_=cnt_row[:]).then_inc(S["sCL"], 16)
            sync.wait_ge(S["sCL"], 32 * e + 16)
            sync.dma_start(out=cnt16[:],
                           in_=cnt_d.rearrange("(c p) -> p c", p=16),
                           allow_slow_non_contiguous=True).then_inc(S["sCL"], 16)
            sync.wait_ge(S["sCR2"], e + 1)
            sync.dma_start(out=cnt_d2, in_=cnt_row[0:1, 0:16]).then_inc(S["sCL2"], 16)
            sync.wait_ge(S["sCL2"], 32 * e + 16)
            sync.dma_start(out=cnt16f[:],
                           in_=cnt_d2.rearrange("(c p) -> p c", p=16)[:, 0:1],
                           allow_slow_non_contiguous=True).then_inc(S["sCL2"], 16)
            sync.wait_ge(S["sCV"], e + 1)
            sync.dma_start(out=idq_d[e], in_=idc16[:]).then_inc(S[f"sIDW{e}"], 16)
            sync.dma_start(out=wrq_d[e].rearrange("(f p) -> p f", p=16),
                           in_=wrs_f[:],
                           allow_slow_non_contiguous=True).then_inc(S[f"sWRW{e}"], 16)
            sync.wait_ge(S[f"sIDW{e}"], 16)
            for gq in range(8):
                sync.dma_start(out=idcq[e][16 * gq:16 * (gq + 1), :],
                               in_=idq_d[e]).then_inc(S[f"sIDR{e}"], 16)
        for k in range(4):
            sync.wait_ge(S[f"sWRW{k}"], 16)
            sync.dma_start(out=wr128[k][:],
                           in_=wrq_d[k].rearrange("(rb p) -> p rb", p=128),
                           allow_slow_non_contiguous=True).then_inc(S[f"sWRL{k}"], 16)
        for k in range(4):
            for fh in range(2):
                for hc in range(16):
                    n = (k * 2 + fh) * 16 + hc
                    if n >= 2:
                        sync.wait_ge(S["sWGC"], n - 1)
                    if fh == 0 and hc == 0 and k > 0:
                        sync.wait_ge(S["sWDC"], 8 * k)
                    sync.dma_start(
                        out=wstg[n % 2][:, 0:1024],
                        in_=wgu[k][128 * hc:128 * (hc + 1),
                                   1024 * fh:1024 * (fh + 1)]
                    ).then_inc(S[f"sWG{n % 2}"], 16)
            for ci in range(8):
                n = k * 8 + ci
                if n >= 2:
                    sync.wait_ge(S["sWDC"], n - 1)
                if ci == 0:
                    sync.wait_ge(S["sWGC"], 32 * (k + 1))
                sync.dma_start(out=wstg[n % 2][:],
                               in_=wdn[k][128 * ci:128 * (ci + 1), :]
                               ).then_inc(S[f"sWD{n % 2}"], 16)
        sync.wait_ge(S["sYW0"], 16 * 16)
        sync.wait_ge(S["sYW1"], 16 * 16)
        if K2DBG:
            for k in range(4):
                sync.dma_start(out=d_idcq[128 * k:128 * (k + 1), :],
                               in_=idcq[k][:]).then_inc(S["sLD"], 16)
                sync.dma_start(out=d_wr[128 * k:128 * (k + 1), :],
                               in_=wr128[k][:]).then_inc(S["sLD"], 16)
            sync.dma_start(out=d_nfc, in_=nfc[:]).then_inc(S["sLD"], 16)
            sync.dma_start(out=d_xg.rearrange("(g p) f -> p g f", p=128),
                           in_=xg[:]).then_inc(S["sLD"], 16)
            sync.dma_start(out=d_buf, in_=buf_fm[:].rearrange("p a b -> p (a b)")
                           ).then_inc(S["sLD"], 16)
            sync.dma_start(out=d_act, in_=act_sb[:].rearrange("p a b -> p (a b)")
                           ).then_inc(S["sLD"], 16)
            sync.wait_ge(S["sLD"], 48 + 16 * 12)

    @block.gpsimd
    def _(g):
        g.iota(tid_i[:], pattern=[[1, 128]], base=0, channel_multiplier=128
               ).then_inc(S["sIOTA"], 1)
        g.iota(io64i[:], pattern=[[16, 64]], base=0, channel_multiplier=1
               ).then_inc(S["sIOTA"], 1)
        g.load_library(libcfg.sparse_gather)
        for e in range(4):
            g.wait_ge(S["sSLC"], 16 * (e + 1))
            g.wait_ge(S["sSLW"], 16 * (e + 1))
            g.wait_ge(S["sMS"], e + 1)
            base = 6 * e
            g.sparse_gather(mid_c[0:16, 0:64], slab_c[0:16, 0:512],
                            num_found=nfa[:]).then_inc(S["sGP"], 1)
            g.wait_ge(S["sGP"], base + 1)
            g.sparse_gather(mid_c[0:16, 64:128], slab_c[0:16, 512:1024],
                            num_found=nfb[:]).then_inc(S["sGP"], 1)
            g.wait_ge(S["sGP"], base + 2)
            g.sparse_gather(mid_w[0:16, 0:64], slab_w[0:16, 0:512],
                            num_found=nfx[:]).then_inc(S["sGP"], 1)
            g.wait_ge(S["sGP"], base + 3)
            g.sparse_gather(mid_w[0:16, 64:128], slab_w[0:16, 512:1024],
                            num_found=nfx[:]).then_inc(S["sGP"], 1)
            g.wait_ge(S["sGP"], base + 4)
            g.engine_nop().then_inc(S["sS1"], 1)
            g.wait_ge(S["sMK"], e + 1)
            g.sparse_gather(idc_f[:], mid_c[:], num_found=nfe[e][:]
                            ).then_inc(S["sGP"], 1)
            g.wait_ge(S["sGP"], base + 5)
            g.sparse_gather(wrs_f[:], mid_w[:], num_found=nfx[:]
                            ).then_inc(S["sGP"], 1)
            g.wait_ge(S["sGP"], base + 6)
            g.engine_nop().then_inc(S["sSG"], 2)
        g.load_library(libcfg.mlp)
        for k in range(4):
            g.wait_ge(S["sCNT"], k + 1)
            c0 = g.value_load(nfc[0:1, 2 * k:2 * k + 1])
            c1 = g.value_load(nfc[0:1, 2 * k + 1:2 * k + 2])
            g.wait_ge(S[f"sIDR{k}"], 16 * 8)
            if k == 0:
                g.wait_ge(S["sBO0"], 16)
                g.wait_ge(S["sBO1"], 16)
                g.wait_ge(S["sSLC"], 64)
                g.wait_ge(S["sSLW"], 64)
                g.wait_ge(S["sXGZ"], 1)
            for half, creg in ((0, c0), (1, c1)):
                G0 = (k * 2 + half) * 4
                if G0 > 0:
                    g.wait_ge(S["sXN2"], G0)
                g.dma_gather(xg[:], x_full, idcq[k][:, 32 * half:32 * (half + 1)],
                             512, creg, 2048).then_inc(S[f"sGA{half}"], 16)
            for r in range(8):
                g.wait_ge(S["sYS"], 32 * k + 4 * (r + 1))
                g.dma_start(out=ytab[1024 * k + 128 * r:1024 * k + 128 * (r + 1), :],
                            in_=ysb[r % 2][:]).then_inc(S[f"sYW{r % 2}"], 16)

    @block.scalar
    def _(scalar):
        cS = [0]
        def sinc(ins):
            cS[0] += 1
            ins.then_inc(S["sSS"], 1)
        def swait():
            scalar.wait_ge(S["sSS"], cS[0])
        for k in range(4):
            for half in range(2):
                for gg in range(4):
                    G = (k * 2 + half) * 4 + gg
                    p = G % 2
                    scalar.wait_ge(S[f"sGA{half}"], 16 * (k + 1))
                    if G >= 2:
                        scalar.wait_ge(S["sVS2"], G - 1)
                    scalar.activation(sq_scr[:], xg[:, gg, :], AF.Square,
                                      accum_out=stc[p][:, 0:1]).then_inc(S["sSQ2"], 1)
                    scalar.wait_ge(S["sVS2"], G + 1)
                    sinc(scalar.activation(stc[p][:, 4:5], stc[p][:, 3:4], AF.Ln))
                    swait()
                    scalar.activation(stc[p][:, 5:6], stc[p][:, 4:5], AF.Exp,
                                      scale=-0.5).then_inc(S["sRS2"], 1)
                    scalar.wait_ge(S["sNB2"], G + 1)
                    if G >= 2:
                        scalar.wait_ge(S["sTP2"], 2 * G - 2)
                    scalar.activation(xng[p][:], xg[:, gg, :], AF.Identity,
                                      bias=stc[p][:, 6:7], scale=stc[p][:, 5:6]
                                      ).then_inc(S["sXN2"], 1)
            for u in range(16):
                scalar.wait_ge(S["sGU"], 32 * k + u + 1)
                if u == 0 and k > 0:
                    scalar.wait_ge(S["sYW0"], 16 * 4 * k)
                    scalar.wait_ge(S["sYW1"], 16 * 4 * k)
                r2, fi = u // 8, u % 8
                scalar.activation(sg_sb[:, fi, 512 * r2:512 * (r2 + 1)],
                                  pgu[u % 2][:], AF.Silu).then_inc(S["sSIL"], 1)
            for u in range(32):
                r, hc = u // 4, u % 4
                scalar.wait_ge(S["sDN"], 32 * k + u + 1)
                scalar.wait_ge(S[f"sWRL{k}"], 16)
                if hc == 0:
                    scalar.wait_ge(S[f"sYW{r % 2}"], 16 * (k * 4 + r // 2))
                scalar.mul(ysb[r % 2][:, 512 * hc:512 * (hc + 1)],
                           pdn[u % 2][:], wr128[k][:, r:r + 1]).then_inc(S["sYS"], 1)

    @block.vector
    def _(vector):
        cV = [0]
        def vinc(ins):
            cV[0] += 1
            ins.then_inc(S["sSV"], 1)
        def vwait():
            vector.wait_ge(S["sSV"], cV[0])
        vector.wait_ge(S["sIOTA"], 2)
        vinc(vector.tensor_copy(tid_f[:], tid_i[:]))
        vwait()
        vinc(vector.tensor_scalar_add(tidp1[:], tid_f[:], 1.0))
        vector.memset(zeros128[:], 0.0)
        vector.wait_ge(S["sLD"], 48)
        vinc(vector.tensor_single_scalar(A_fe[:], routS[:], 0.0, ALU.is_gt))
        vwait()
        vinc(vector.tensor_reduce(rs32[:], A_fe[:].rearrange("p f e -> p e f"),
                                  axis=AX, op=ALU.add))
        vwait()
        vector.tensor_copy(rs16[:], rs32[:]).then_inc(S["sRS16"], 1)
        vector.wait_ge(S["sPRE"], 1)
        vinc(vector.tensor_copy(pref[:], ppre[:]))
        vwait()
        for e in range(4):
            vinc(vector.tensor_tensor_scan(
                posb[:, e, :], A_fe[:, :, e], zeros128[:],
                pref[:, e:e + 1], ALU.add, ALU.add))
        vwait()
        vinc(vector.tensor_sub(posb[:], posb[:],
                               A_fe[:].rearrange("p f e -> p e f")))
        vwait()
        vinc(vector.tensor_single_scalar(keepb[:], posb[:], 1024.0, ALU.is_lt))
        vwait()
        vinc(vector.tensor_mul(keepb[:], keepb[:],
                               A_fe[:].rearrange("p f e -> p e f")))
        vwait()
        for e in range(4):
            vinc(vector.tensor_mul(cida[:, e, :], keepb[:, e, :], tidp1[:]))
            vinc(vector.tensor_mul(wrca[:, e, :], keepb[:, e, :], routS[:, :, e]))
        vwait()
        vinc(vector.tensor_scalar_add(cida[:].rearrange("p a b -> p (a b)"),
                                      cida[:].rearrange("p a b -> p (a b)"), -1.0))
        vwait()
        vinc(vector.tensor_add(wrca[:], wrca[:], keepb[:]))
        vwait()
        vector.tensor_scalar_add(wrca[:].rearrange("p a b -> p (a b)"),
                                 wrca[:].rearrange("p a b -> p (a b)"), -1.0
                                 ).then_inc(S["sCID"], 1)
        vinc(vector.memset(ones16[:], 1.0))
        vector.wait_ge(S["sIOTA"], 2)
        vinc(vector.tensor_copy(io64[:], io64i[:]))
        for e in range(4):
            if e > 0:
                vector.wait_ge(S[f"sIDW{e - 1}"], 16)
                vector.wait_ge(S[f"sWRW{e - 1}"], 16)
            vinc(vector.memset(idc_f[:], -1.0))
            vwait()
            vinc(vector.memset(mid_c[:], -1.0))
            vwait()
            vinc(vector.memset(mid_w[:], -1.0))
            vwait()
            vector.memset(wrs_f[:], 0.0).then_inc(S["sMS"], 1)
            vector.wait_ge(S["sS1"], e + 1)
            vinc(vector.tensor_copy(nf_f[0:1, 0:1], nfa[:]))
            vinc(vector.tensor_copy(nf_f[0:1, 1:2], nfb[:]))
            vwait()
            vinc(vector.tensor_single_scalar(cnt_row[0:1, 0:16], ones16[0:1, 0:16],
                                             nf_f[0:1, 0:1], ALU.mult))
            vwait()
            vector.tensor_single_scalar(cnt_row[0:1, 16:32], ones16[0:1, 0:16],
                                        nf_f[0:1, 1:2], ALU.mult
                                        ).then_inc(S["sCR"], 1)
            vector.wait_ge(S["sCL"], 32 * e + 32)
            vinc(vector.tensor_single_scalar(mska[:], io64[:], cnt16[:, 0:1],
                                             ALU.is_lt))
            vwait()
            vinc(vector.tensor_single_scalar(mskb[:], io64[:], cnt16[:, 1:2],
                                             ALU.is_lt))
            vwait()
            vinc(vector.tensor_scalar_add(tmsk[:], mid_c[0:16, 0:64], 1.0))
            vwait()
            vinc(vector.tensor_mul(tmsk[:], tmsk[:], mska[:]))
            vwait()
            vinc(vector.tensor_scalar_add(mid_c[0:16, 0:64], tmsk[:], -1.0))
            vwait()
            vinc(vector.tensor_scalar_add(tmsk[:], mid_c[0:16, 64:128], 1.0))
            vwait()
            vinc(vector.tensor_mul(tmsk[:], tmsk[:], mskb[:]))
            vwait()
            vinc(vector.tensor_scalar_add(mid_c[0:16, 64:128], tmsk[:], -1.0))
            vwait()
            vinc(vector.tensor_scalar_add(tmsk[:], mid_w[0:16, 0:64], 1.0))
            vwait()
            vinc(vector.tensor_mul(tmsk[:], tmsk[:], mska[:]))
            vwait()
            vinc(vector.tensor_scalar_add(mid_w[0:16, 0:64], tmsk[:], -1.0))
            vwait()
            vinc(vector.tensor_scalar_add(tmsk[:], mid_w[0:16, 64:128], 1.0))
            vwait()
            vinc(vector.tensor_mul(tmsk[:], tmsk[:], mskb[:]))
            vwait()
            vinc(vector.tensor_scalar_add(mid_w[0:16, 64:128], tmsk[:], -1.0))
            vwait()
            vector.engine_nop().then_inc(S["sMK"], 1)
            vector.wait_ge(S["sSG"], 2 * (e + 1))
            vinc(vector.tensor_copy(nf_f[0:1, 2:3], nfe[e][:]))
            vwait()
            vector.tensor_single_scalar(cnt_row[0:1, 0:16], ones16[0:1, 0:16],
                                        nf_f[0:1, 2:3], ALU.mult
                                        ).then_inc(S["sCR2"], 1)
            vector.wait_ge(S["sCL2"], 32 * (e + 1))
            vinc(vector.tensor_single_scalar(mska[:], io64[:], cnt16f[:, 0:1],
                                             ALU.is_lt))
            vwait()
            vinc(vector.tensor_scalar_add(tmsk[:], idc_f[:], 1.0))
            vwait()
            vinc(vector.tensor_mul(tmsk[:], tmsk[:], mska[:]))
            vwait()
            vinc(vector.tensor_scalar_add(idc_f[:], tmsk[:], -1.0))
            vwait()
            vinc(vector.tensor_mul(wrs_f[:], wrs_f[:], mska[:]))
            vwait()
            vinc(vector.tensor_scalar_max(idc_f[0:1, 0:1], idc_f[0:1, 0:1], 0.0))
            vwait()
            vinc(vector.tensor_single_scalar(idc_f[0:1, 0:1], idc_f[0:1, 0:1],
                                             16383.0, ALU.min))
            vwait()
            vinc(vector.tensor_scalar_max(idc_f[0:1, 32:33], idc_f[0:1, 32:33], 0.0))
            vwait()
            vinc(vector.tensor_single_scalar(idc_f[0:1, 32:33], idc_f[0:1, 32:33],
                                             16383.0, ALU.min))
            vwait()
            vinc(vector.tensor_scalar_max(wrs_f[:], wrs_f[:], 0.0))
            vwait()
            vinc(vector.tensor_copy(idc16[:], idc_f[:]))
            vinc(vector.tensor_copy(nf_f[0:1, 2 * e + 4:2 * e + 5], nf_f[0:1, 2:3]))
            vwait()
            co = 2 * e + 4
            vinc(vector.tensor_single_scalar(nf_f[0:1, co + 1:co + 2],
                                             nf_f[0:1, co:co + 1], 512.0,
                                             ALU.subtract))
            vwait()
            vinc(vector.tensor_single_scalar(nf_f[0:1, co:co + 1],
                                             nf_f[0:1, co:co + 1], 512.0,
                                             ALU.min))
            vwait()
            vinc(vector.tensor_single_scalar(nf_f[0:1, co:co + 2],
                                             nf_f[0:1, co:co + 2], 1.0,
                                             ALU.max))
            vwait()
            vinc(vector.tensor_single_scalar(nf_f[0:1, co + 1:co + 2],
                                             nf_f[0:1, co + 1:co + 2], 512.0,
                                             ALU.min))
            vwait()
            vinc(vector.tensor_copy(nfc[0:1, 2 * e:2 * e + 2],
                                    nf_f[0:1, co:co + 2]))
            vwait()
            vector.engine_nop().then_inc(S["sCV"], 1)
            vector.engine_nop().then_inc(S["sCNT"], 1)
        vector.wait_ge(S["sBO0"], 16)
        vector.wait_ge(S["sBO1"], 16)
        vector.memset(xg[:], 0.0).then_inc(S["sXGZ"], 1)
        for k in range(4):
            for half in range(2):
                for gg in range(4):
                    G = (k * 2 + half) * 4 + gg
                    p = G % 2
                    vector.wait_ge(S[f"sGA{half}"], 16 * (k + 1))
                    if G >= 2:
                        vector.wait_ge(S["sXN2"], G - 1)
                    vinc(vector.reduce_sum(stc[p][:, 1:2], xg[:, gg, :], axis=AX,
                                           negate=True))
                    vwait()
                    vinc(vector.tensor_scalar_mul(stc[p][:, 1:2], stc[p][:, 1:2],
                                                  1.0 / 2048.0))
                    vector.wait_ge(S["sSQ2"], G + 1)
                    vinc(vector.tensor_scalar_mul(stc[p][:, 2:3], stc[p][:, 0:1],
                                                  1.0 / 2048.0))
                    vwait()
                    vinc(vector.tensor_mul(stc[p][:, 3:4], stc[p][:, 1:2],
                                           stc[p][:, 1:2]))
                    vwait()
                    vinc(vector.tensor_sub(stc[p][:, 3:4], stc[p][:, 2:3],
                                           stc[p][:, 3:4]))
                    vwait()
                    vector.tensor_scalar_add(stc[p][:, 3:4], stc[p][:, 3:4], 1e-5
                                             ).then_inc(S["sVS2"], 1)
                    vector.wait_ge(S["sRS2"], G + 1)
                    vwait()
                    vector.tensor_mul(stc[p][:, 6:7], stc[p][:, 1:2],
                                      stc[p][:, 5:6]).then_inc(S["sNB2"], 1)
                    for g2 in range(2):
                        tg = 2 * G + g2
                        vector.wait_ge(S["sTP2"], tg + 1)
                        vector.tensor_copy(
                            buf_fm[:, 8 * g2:8 * (g2 + 1),
                                   512 * half + 128 * gg:512 * half + 128 * (gg + 1)],
                            tr[tg % 2][:].rearrange("p (j t2) -> p j t2", t2=128),
                        ).then_inc(S["sTE2"], 1)
            for fh in range(2):
                for hc in range(16):
                    n = (k * 2 + fh) * 16 + hc
                    vector.wait_ge(S[f"sWG{n % 2}"], 16 * (n // 2 + 1))
                    vector.wait_ge(S["sGU"], 32 * k + 16 * fh)
                    vector.tensor_copy(wgur[:, hc, :], wstg[n % 2][:, 0:1024]
                                       ).then_inc(S["sWGC"], 1)
            for u in range(16):
                r2, fi = u // 8, u % 8
                vector.wait_ge(S["sGU"], 32 * k + 16 + u + 1)
                vector.wait_ge(S["sSIL"], 16 * (k + 1))
                vector.tensor_mul(act_sb[:, fi, 512 * r2:512 * (r2 + 1)],
                                  sg_sb[:, fi, 512 * r2:512 * (r2 + 1)],
                                  pgu[u % 2][:]).then_inc(S["sACT"], 1)
            for ci in range(8):
                n = k * 8 + ci
                vector.wait_ge(S[f"sWD{n % 2}"], 16 * (n // 2 + 1))
                vector.wait_ge(S["sDN"], 32 * k)
                vector.tensor_copy(wdnr[:, ci, :], wstg[n % 2][:]
                                   ).then_inc(S["sWDC"], 1)

    @block.tensor
    def _(tensor):
        tensor.wait_ge(S["sLD"], 48)
        tensor.wait_ge(S["sRS16"], 1)
        tensor.matmul(ppre[:], s_tri[:], rs16[:], start=True, stop=True
                      ).then_inc(S["sPRE"], 1)
        for k in range(4):
            for half in range(2):
                for gg in range(4):
                    G = (k * 2 + half) * 4 + gg
                    tensor.wait_ge(S["sXN2"], G + 1)
                    for g2 in range(2):
                        tg = 2 * G + g2
                        if tg >= 2:
                            tensor.wait_ge(S["sTE2"], tg - 1)
                        for j in range(8):
                            jj = 8 * g2 + j
                            ins = tensor.transpose(
                                tr[tg % 2][:, 128 * j:128 * (j + 1)],
                                xng[G % 2][:, 128 * jj:128 * (jj + 1)],
                                s_id[:])
                        ins.then_inc(S["sTP2"], 1)
            tensor.wait_ge(S["sTE2"], 16 * (k + 1))
            def gu_bank_free(U):
                if U < 0:
                    return
                kp, up = divmod(U, 32)
                if up < 16:
                    tensor.wait_ge(S["sSIL"], 16 * kp + up + 1)
                else:
                    tensor.wait_ge(S["sACT"], 16 * kp + (up - 16) + 1)
            for fh in range(2):
                tensor.wait_ge(S["sWGC"], (2 * k + fh + 1) * 16)
                for u in range(16):
                    r2, fi = u // 8, u % 8
                    uu = 16 * fh + u
                    gu_bank_free(32 * k + uu - 2)
                    for hc in range(16):
                        ins = tensor.matmul(
                            pgu[uu % 2][:],
                            wgur[:, hc, 128 * fi:128 * (fi + 1)],
                            buf_fm[:, hc, 512 * r2:512 * (r2 + 1)],
                            start=(hc == 0), stop=(hc == 15))
                    ins.then_inc(S["sGU"], 1)
            tensor.wait_ge(S["sACT"], 16 * (k + 1))
            tensor.wait_ge(S["sWDC"], 8 * (k + 1))
            for u in range(32):
                r, hc = u // 4, u % 4
                if 32 * k + u - 1 > 0:
                    tensor.wait_ge(S["sYS"], 32 * k + u - 1)
                for ci in range(8):
                    ins = tensor.matmul(
                        pdn[u % 2][:],
                        act_sb[:, ci, 128 * r:128 * (r + 1)],
                        wdnr[:, ci, 512 * hc:512 * (hc + 1)],
                        start=(ci == 0), stop=(ci == 7))
                ins.then_inc(S["sDN"], 1)
    return nc


def make_k2():
    from contextlib import ExitStack
    nc = bass.Bass("TRN2", target_bir_lowering=False)
    with ExitStack() as ctx:
        build_k2(nc, ctx.enter_context)
    from concourse.library_overlay import lower_extended_insts
    lower_extended_insts(nc)
    return nc


def k2_inputs(x_flat, wr_full, w_gu, w_dn, core):
    import numpy as np
    import ml_dtypes
    return dict(
        x_full=np.ascontiguousarray(x_flat.astype(np.float32)),
        rout=np.ascontiguousarray(wr_full[:, 4 * core:4 * (core + 1)].astype(np.float32)),
        wgu=np.ascontiguousarray(w_gu[4 * core:4 * (core + 1)].astype(np.float32)),
        wdn=np.ascontiguousarray(w_dn[4 * core:4 * (core + 1)].astype(np.float32)),
        ident=np.eye(128, dtype=ml_dtypes.bfloat16),
        triu=np.triu(np.ones((128, 128), np.float32), 1).astype(ml_dtypes.bfloat16),
    )



"""K3 v2: range-based combine. Core c owns tokens [2048c, +2048).

out[t] = x[t] + h[t] + sum over kept routes (t, e) of ytab[slot(t, e)]
where ytab rows are already scaled by w_r * f (K1 folds f into wr, K2
scales y rows). Slots of expert e serving own tokens form the contiguous
ytab row range [1024e + lo_e, 1024e + lo_e + nf_e): slots are assigned in
global token order, so lo_e = clamp(#routed-before-own-range, 0, 1024) and
own kept tokens are the first nf_e own-routed tokens.

Pipeline: routing mask -> capacity scan -> per-expert compaction of own
kept token ids (sparse_gather) -> per-expert dma_gather of the contiguous
row range + dma_scatter_add into the f32 output (base = x + h written
first). Scatters are serialized (cross-call duplicate rows accumulate via
completion-sem chaining; within one call targets are distinct tokens).
"""
import concourse.bass as bass
import concourse.mybir as mybir
from concourse import library_config as libcfg

F32 = mybir.dt.float32
BF16 = mybir.dt.bfloat16
I32 = mybir.dt.int32
I16 = mybir.dt.int16
U32 = mybir.dt.uint32
ALU = mybir.AluOpType
AX = mybir.AxisListType.X


def build_k3(nc: bass.Bass, E):
    DBG = 3
    ytab = nc.dram_tensor("ytab", [32768, 2048], F32, kind="ExternalInput").ap()
    rout = nc.dram_tensor("rout", [16384, 32], F32, kind="ExternalInput").ap()
    rown = nc.dram_tensor("rown", [2048, 32], F32, kind="ExternalInput").ap()
    x_own = nc.dram_tensor("x_own", [2048, 2048], F32, kind="ExternalInput").ap()
    h_own = nc.dram_tensor("h_own", [2048, 2048], BF16, kind="ExternalInput").ap()
    mpre = nc.dram_tensor("mpre", [128, 128], BF16, kind="ExternalInput").ap()
    triu = nc.dram_tensor("triu", [128, 128], BF16, kind="ExternalInput").ap()
    tof3 = nc.dram_tensor("tof3", [128, 512], F32, kind="ExternalInput").ap()
    out_own = nc.dram_tensor("out_own", [2049, 2048], F32,
                             kind="ExternalOutput").ap()
    d_nfc = nc.dram_tensor("d_nfc", [1, 64], U32, kind="ExternalOutput").ap()
    d_si = nc.dram_tensor("d_si", [16, 2048], I16, kind="ExternalOutput").ap()
    d_gi = nc.dram_tensor("d_gi", [16, 2048], I16, kind="ExternalOutput").ap()
    d_sgo = nc.dram_tensor("d_sgo", [16, 2048], F32, kind="ExternalOutput").ap()
    cid_d = nc.dram_tensor("cid_d", [32, 2048], F32, kind="Internal").ap()
    cnt_d = nc.dram_tensor("cnt_d", [16, 32], F32, kind="Internal").ap()
    sidx_d = nc.dram_tensor("sidx_d", [16, 2048], I16, kind="Internal").ap()
    gidx_d = nc.dram_tensor("gidx_d", [16, 2048], I16, kind="Internal").ap()

    s_mpre = E(nc.sbuf_tensor("s_mpre", [128, 128], BF16))
    s_tri = E(nc.sbuf_tensor("s_tri", [128, 128], BF16))
    s_tof = E(nc.sbuf_tensor("s_tof", [128, 32, 16], F32))
    rfull = E(nc.sbuf_tensor("rfull", [128, 128, 32], F32))
    a16 = E(nc.sbuf_tensor("a16", [128, 128, 32], BF16))
    rS = E(nc.sbuf_tensor("rS", [128, 16, 32], F32))
    aown = E(nc.sbuf_tensor("aown", [128, 32, 16], F32))
    rsum = E(nc.sbuf_tensor("rsum", [128, 32], F32))
    rsum16 = E(nc.sbuf_tensor("rsum16", [128, 32], BF16))
    asum = E(nc.sbuf_tensor("asum", [128, 32], F32))
    asum16 = E(nc.sbuf_tensor("asum16", [128, 32], BF16))
    prefx = E(nc.sbuf_tensor("prefx", [128, 32], F32))
    basef = E(nc.sbuf_tensor("basef", [128, 32], F32))
    baseg = E(nc.sbuf_tensor("baseg", [128, 32], F32))
    poso = E(nc.sbuf_tensor("poso", [128, 32, 16], F32))
    kepto = E(nc.sbuf_tensor("kepto", [128, 32, 16], F32))
    cidk = E(nc.sbuf_tensor("cidk", [128, 32, 16], F32))
    zer16 = E(nc.sbuf_tensor("zer16", [128, 16], F32))
    zrow = E(nc.sbuf_tensor("zrow", [1, 2048], F32))
    slab = [E(nc.sbuf_tensor(f"slab{i}", [16, 128], F32)) for i in range(2)]
    sgo_all = E(nc.sbuf_tensor("sgo_all", [16, 32, 64], F32))
    sgo = [E(nc.sbuf_tensor(f"sgo{i}", [16, 64], F32)) for i in range(2)]
    nfu = [E(nc.sbuf_tensor(f"nfu{i}", [1, 1], U32)) for i in range(2)]
    io64i = E(nc.sbuf_tensor("io64i", [16, 64], I32))
    io64 = E(nc.sbuf_tensor("io64", [16, 64], F32))
    nf_f = E(nc.sbuf_tensor("nf_f", [1, 32], F32))
    nfc_f = E(nc.sbuf_tensor("nfc_f", [1, 64], F32))
    nfc = E(nc.sbuf_tensor("nfc", [1, 64], U32))
    cnt16 = E(nc.sbuf_tensor("cnt16", [16, 32], F32))
    msk = E(nc.sbuf_tensor("msk", [16, 64], F32))
    tmp1 = E(nc.sbuf_tensor("tmp1", [16, 64], F32))
    sgi32 = E(nc.sbuf_tensor("sgi32", [16, 64], I32))
    sgc = E(nc.sbuf_tensor("sgc", [16, 64], F32))
    scr = E(nc.sbuf_tensor("scr", [1, 8], F32))
    sidx_f = E(nc.sbuf_tensor("sidx_f", [16, 32, 64], F32))
    gidx_f = E(nc.sbuf_tensor("gidx_f", [16, 32, 64], F32))
    si16 = E(nc.sbuf_tensor("si16", [16, 32, 64], I16))
    gi16 = E(nc.sbuf_tensor("gi16", [16, 32, 64], I16))
    si_all = E(nc.sbuf_tensor("si_all", [128, 32, 64], I16))
    gi_all = E(nc.sbuf_tensor("gi_all", [128, 32, 64], I16))
    gbuf = [E(nc.sbuf_tensor(f"gbuf{i}", [128, 4, 2048], F32)) for i in range(2)]
    xt = [E(nc.sbuf_tensor(f"xt{i}", [128, 2048], F32)) for i in range(2)]
    ht = [E(nc.sbuf_tensor(f"ht{i}", [128, 2048], BF16)) for i in range(2)]
    ot = [E(nc.sbuf_tensor(f"ot{i}", [128, 2048], F32)) for i in range(2)]

    pb = E(nc.psum_tensor("pb", [128, 32], F32))
    pb2 = E(nc.psum_tensor("pb2", [128, 32], F32))

    sem_names = ["sLD", "sAS", "sRM16", "sPB", "sBB", "sIO",
                 "sCK", "sCW", "sSL0", "sSL1", "sSG", "sGP", "sGC", "sNF", "sNC", "sCS",
                 "sCL", "sIC", "sIW", "sIR", "sXH0", "sXH1", "sOT", "sST0",
                 "sST1", "sGA", "sSC", "sDBG", "sGBZ", "sZR", "sZD", "sSV"]
    S = {n: E(nc.semaphore(n)) for n in sem_names}
    block = E(nc.Block())

    obase = out_own[0:2048, :].rearrange("(p f) d -> p f d", f=16)

    @block.sync
    def _(sync):
        for (d, s) in [(s_mpre, mpre), (s_tri, triu)]:
            sync.dma_start(out=d[:], in_=s).then_inc(S["sLD"], 16)
        sync.dma_start(out=s_tof[:],
                       in_=tof3.rearrange("p (e f) -> p e f", f=16)
                       ).then_inc(S["sLD"], 16)
        sync.dma_start(out=rfull[:],
                       in_=rout.rearrange("(p q) e -> p q e", p=128)
                       ).then_inc(S["sLD"], 16)
        sync.dma_start(out=rS[:],
                       in_=rown.rearrange("(p f) e -> p f e", p=128)
                       ).then_inc(S["sLD"], 16)
        if DBG < 2:
            return
        sync.wait_ge(S["sCK"], 1)
        sync.dma_start(out=cid_d.rearrange("e (p f) -> p e f", p=128),
                       in_=cidk[:]).then_inc(S["sCW"], 16)
        sync.wait_ge(S["sCW"], 16)
        for e in range(32):
            if e >= 2:
                sync.wait_ge(S["sSG"], e - 1)
            sync.dma_start(out=slab[e % 2][:],
                           in_=cid_d[e].rearrange("(f p) -> p f", p=16),
                           allow_slow_non_contiguous=True
                           ).then_inc(S[f"sSL{e % 2}"], 16)
        sync.wait_ge(S["sNF"], 1)
        for r in range(16):
            sync.dma_start(out=cnt_d[r:r + 1, :], in_=nf_f[:]
                           ).then_inc(S["sCS"], 16)
        sync.wait_ge(S["sCS"], 256)
        sync.dma_start(out=cnt16[:], in_=cnt_d).then_inc(S["sCL"], 16)
        sync.wait_ge(S["sIC"], 1)
        sync.dma_start(out=sidx_d.rearrange("p (e f) -> p e f", f=64),
                       in_=si16[:]).then_inc(S["sIW"], 16)
        sync.dma_start(out=gidx_d.rearrange("p (e f) -> p e f", f=64),
                       in_=gi16[:]).then_inc(S["sIW"], 16)
        sync.wait_ge(S["sIW"], 32)
        for g8 in range(8):
            sync.dma_start(out=si_all[16 * g8:16 * (g8 + 1), :, :],
                           in_=sidx_d.rearrange("p (e f) -> p e f", f=64)
                           ).then_inc(S["sIR"], 16)
            sync.dma_start(out=gi_all[16 * g8:16 * (g8 + 1), :, :],
                           in_=gidx_d.rearrange("p (e f) -> p e f", f=64)
                           ).then_inc(S["sIR"], 16)
        sync.dma_start(out=d_nfc, in_=nfc[:]).then_inc(S["sDBG"], 16)
        sync.dma_start(out=d_si.rearrange("p (e f) -> p e f", f=64),
                       in_=si16[:]).then_inc(S["sDBG"], 16)
        sync.dma_start(out=d_gi.rearrange("p (e f) -> p e f", f=64),
                       in_=gi16[:]).then_inc(S["sDBG"], 16)
        sync.dma_start(out=d_sgo.rearrange("p (e f) -> p e f", f=64),
                       in_=sgo_all[:]).then_inc(S["sDBG"], 16)
        sync.wait_ge(S["sDBG"], 64)

    @block.scalar
    def _(scalar):
        for q in range(16):
            if q >= 2:
                scalar.wait_ge(S["sOT"], q - 1)
            scalar.dma_start(out=xt[q % 2][:],
                             in_=x_own.rearrange("(p f) d -> p f d", f=16)[:, q, :]
                             ).then_inc(S[f"sXH{q % 2}"], 16)
            scalar.dma_start(out=ht[q % 2][:],
                             in_=h_own.rearrange("(p f) d -> p f d", f=16)[:, q, :]
                             ).then_inc(S[f"sXH{q % 2}"], 16)
            if q >= 1:
                scalar.wait_ge(S["sOT"], q)
                scalar.dma_start(out=obase[:, q - 1, :], in_=ot[(q - 1) % 2][:]
                                 ).then_inc(S[f"sST{(q - 1) % 2}"], 16)
        scalar.wait_ge(S["sOT"], 16)
        scalar.dma_start(out=obase[:, 15, :], in_=ot[1][:]
                         ).then_inc(S["sST1"], 16)
        scalar.wait_ge(S["sZR"], 1)
        scalar.dma_start(out=out_own[2048:2049, :], in_=zrow[:]
                         ).then_inc(S["sZD"], 16)

    @block.tensor
    def _(tensor):
        tensor.wait_ge(S["sLD"], 80)
        tensor.wait_ge(S["sRM16"], 1)
        tensor.matmul(pb[:], s_tri[:], rsum16[:], start=True, stop=True
                      ).then_inc(S["sPB"], 1)
        tensor.wait_ge(S["sAS"], 1)
        tensor.matmul(pb2[:], s_mpre[:], asum16[:], start=True, stop=True
                      ).then_inc(S["sBB"], 1)

    @block.vector
    def _(vector):
        cV = [0]
        def vinc(ins):
            cV[0] += 1
            ins.then_inc(S["sSV"], 1)
        def vwait():
            vector.wait_ge(S["sSV"], cV[0])
        vector.memset(gbuf[0][:], 0.0).then_inc(S["sGBZ"], 1)
        vector.memset(gbuf[1][:], 0.0).then_inc(S["sGBZ"], 1)
        vector.memset(zrow[:], 0.0).then_inc(S["sZR"], 1)
        vector.wait_ge(S["sLD"], 80)
        vinc(vector.memset(zer16[:], 0.0))
        vinc(vector.tensor_single_scalar(a16[:], rfull[:], 0.0, ALU.is_gt))
        vwait()
        vinc(vector.tensor_reduce(asum[:], a16[:].rearrange("p q e -> p e q"),
                                  axis=AX, op=ALU.add))
        vwait()
        vector.tensor_copy(asum16[:], asum[:]).then_inc(S["sAS"], 1)
        vinc(vector.tensor_single_scalar(aown[:], rS[:].rearrange("p f e -> p e f"),
                                         0.0, ALU.is_gt))
        vwait()
        vinc(vector.tensor_reduce(rsum[:], aown[:], axis=AX, op=ALU.add))
        vwait()
        vector.tensor_copy(rsum16[:], rsum[:]).then_inc(S["sRM16"], 1)
        if DBG >= 2:
            vector.wait_ge(S["sIO"], 1)
            vinc(vector.tensor_copy(io64[:], io64i[:]))
        vector.wait_ge(S["sPB"], 1)
        vinc(vector.tensor_copy(prefx[:], pb[:]))
        vwait()
        vector.wait_ge(S["sBB"], 1)
        vinc(vector.tensor_add(basef[:], prefx[:], pb2[:]))
        vinc(vector.tensor_single_scalar(baseg[:], pb2[:], 1024.0, ALU.min))
        vwait()
        for e in range(32):
            vinc(vector.tensor_tensor_scan(poso[:, e, :], aown[:, e, :],
                                           zer16[:], basef[:, e:e + 1],
                                           ALU.add, ALU.add))
        vwait()
        vinc(vector.tensor_sub(poso[:], poso[:], aown[:]))
        vwait()
        vinc(vector.tensor_single_scalar(kepto[:], poso[:], 1024.0, ALU.is_lt))
        vwait()
        vinc(vector.tensor_mul(kepto[:], kepto[:], aown[:]))
        vwait()
        vinc(vector.tensor_mul(cidk[:], kepto[:], s_tof[:]))
        vwait()
        vinc(vector.tensor_scalar_add(cidk[:], cidk[:], -1.0))
        vwait()
        vector.engine_nop().then_inc(S["sCK"], 1)
        for q in range(16):
            vector.wait_ge(S[f"sXH{q % 2}"], 32 * (q // 2 + 1))
            if q >= 2:
                vector.wait_ge(S[f"sST{q % 2}"], 16 * (q // 2))
            vector.tensor_add(ot[q % 2][:], xt[q % 2][:], ht[q % 2][:]
                              ).then_inc(S["sOT"], 1)
        if DBG < 2:
            return
        for e in range(32):
            vector.wait_ge(S["sSG"], e + 1)
            vinc(vector.tensor_copy(sgo_all[:, e, :], sgo[e % 2][:]))
            vinc(vector.tensor_copy(nf_f[0:1, e:e + 1], nfu[e % 2][:]))
            vwait()
            vector.engine_nop().then_inc(S["sGC"], 1)
        vinc(vector.tensor_single_scalar(nfc_f[0:1, 0:32], nf_f[:],
                                         512.0, ALU.min))
        vinc(vector.tensor_single_scalar(nfc_f[0:1, 32:64], nf_f[:],
                                         512.0, ALU.subtract))
        vwait()
        vinc(vector.tensor_single_scalar(nfc_f[0:1, 32:64], nfc_f[0:1, 32:64],
                                         512.0, ALU.min))
        vwait()
        vinc(vector.tensor_single_scalar(nfc_f[:], nfc_f[:], 1.0, ALU.max))
        vwait()
        vinc(vector.tensor_copy(nfc[:], nfc_f[:]))
        vwait()
        vector.engine_nop().then_inc(S["sNC"], 1)
        vector.engine_nop().then_inc(S["sNF"], 1)
        vector.wait_ge(S["sCL"], 16)
        for e in range(32):
            vinc(vector.tensor_single_scalar(msk[:], io64[:],
                                             cnt16[:, e:e + 1], ALU.is_lt))
            vwait()
            vinc(vector.tensor_scalar_add(tmp1[:], io64[:],
                                          float(1024 * e + 1)))
            vwait()
            vinc(vector.tensor_single_scalar(tmp1[:], tmp1[:],
                                             baseg[0:16, e:e + 1], ALU.add))
            vwait()
            vinc(vector.tensor_mul(tmp1[:], tmp1[:], msk[:]))
            vwait()
            vinc(vector.tensor_scalar_add(gidx_f[:, e, :], tmp1[:], -1.0))
            vwait()
            vinc(vector.tensor_copy(sgi32[:], sgo_all[:, e, :]))
            vwait()
            vinc(vector.tensor_copy(sgc[:], sgi32[:]))
            vwait()
            vinc(vector.tensor_scalar_add(tmp1[:], sgc[:], 1.0))
            vwait()
            vinc(vector.tensor_mul(tmp1[:], tmp1[:], msk[:]))
            vwait()
            vinc(vector.tensor_scalar_add(sidx_f[:, e, :], tmp1[:], -1.0))
            vwait()
            vinc(vector.tensor_single_scalar(scr[0:1, 0:1], nf_f[0:1, e:e + 1],
                                             0.5, ALU.is_lt))
            vinc(vector.tensor_single_scalar(scr[0:1, 1:2], nf_f[0:1, e:e + 1],
                                             512.5, ALU.is_lt))
            vwait()
            vinc(vector.tensor_scalar_mul(scr[0:1, 2:3], scr[0:1, 0:1], 2049.0))
            vinc(vector.tensor_scalar_mul(scr[0:1, 3:4], scr[0:1, 1:2], 2049.0))
            vwait()
            vinc(vector.tensor_single_scalar(sidx_f[0:1, e, 0:1],
                                             sidx_f[0:1, e, 0:1],
                                             scr[0:1, 2:3], ALU.add))
            vinc(vector.tensor_single_scalar(gidx_f[0:1, e, 0:1],
                                             gidx_f[0:1, e, 0:1],
                                             scr[0:1, 0:1], ALU.add))
            vinc(vector.tensor_single_scalar(sidx_f[0:1, e, 32:33],
                                             sidx_f[0:1, e, 32:33],
                                             scr[0:1, 3:4], ALU.add))
            vinc(vector.tensor_single_scalar(gidx_f[0:1, e, 32:33],
                                             gidx_f[0:1, e, 32:33],
                                             scr[0:1, 1:2], ALU.add))
            vwait()
        vinc(vector.tensor_copy(si16[:], sidx_f[:]))
        vinc(vector.tensor_copy(gi16[:], gidx_f[:]))
        vwait()
        vector.engine_nop().then_inc(S["sIC"], 1)

    @block.gpsimd
    def _(g):
        if DBG < 2:
            return
        g.iota(io64i[:], pattern=[[16, 64]], base=0, channel_multiplier=1
               ).then_inc(S["sIO"], 1)
        g.load_library(libcfg.sparse_gather)
        for e in range(32):
            g.wait_ge(S[f"sSL{e % 2}"], 16 * (e // 2 + 1))
            if e >= 2:
                g.wait_ge(S["sGC"], e - 1)
            g.sparse_gather(sgo[e % 2][:], slab[e % 2][:],
                            num_found=nfu[e % 2][:]).then_inc(S["sGP"], 1)
            g.wait_ge(S["sGP"], e + 1)
            g.engine_nop().then_inc(S["sSG"], 1)
        if DBG < 3:
            return
        g.load_library(libcfg.mlp)
        g.wait_ge(S["sNC"], 1)
        g.wait_ge(S["sIR"], 256)
        cr = g.alloc_register()
        for j in range(64):
            e, hh = j // 2, j % 2
            g.reg_load(cr, nfc[0:1, 32 * hh + e:32 * hh + e + 1])
            if j == 0:
                g.wait_ge(S["sGBZ"], 2)
            if j >= 2 and DBG != 4:
                g.wait_ge(S["sSC"], 16 * (j - 1))
            g.dma_gather(gbuf[j % 2][:], ytab, gi_all[:, e, 32 * hh:32 * hh + 32],
                         512, cr, 2048).then_inc(S["sGA"], 16)
            g.wait_ge(S["sGA"], 16 * (j + 1))
            if DBG == 4:
                continue
            if j == 0:
                g.wait_ge(S["sST0"], 16 * 8)
                g.wait_ge(S["sST1"], 16 * 8)
                g.wait_ge(S["sZD"], 16)
            else:
                g.wait_ge(S["sSC"], 16 * j)
            g.dma_scatter_add(out_own, gbuf[j % 2][:],
                              si_all[:, e, 32 * hh:32 * hh + 32],
                              512, cr, 2048, elem_step=2048
                              ).then_inc(S["sSC"], 16)
        if DBG != 4:
            g.wait_ge(S["sSC"], 16 * 64)
    return nc


def make_k3():
    from contextlib import ExitStack
    from concourse.library_overlay import lower_extended_insts
    nc = bass.Bass("TRN2", target_bir_lowering=False)
    with ExitStack() as ctx:
        build_k3(nc, ctx.enter_context)
    lower_extended_insts(nc)
    return nc


def k3_inputs(ytab_full, rout_full, x_flat, h_full, core):
    import numpy as np
    import ml_dtypes
    BF = ml_dtypes.bfloat16
    mpre = np.tile((np.arange(128) < 16 * core).astype(np.float32)[:, None],
                   (1, 128)).astype(BF)
    tof3 = np.tile((16.0 * np.arange(128, dtype=np.float32)[:, None]
                    + np.arange(16, dtype=np.float32)[None, :] + 1.0), (1, 32))
    return dict(
        ytab=np.ascontiguousarray(ytab_full.astype(np.float32)),
        rout=np.ascontiguousarray(rout_full.astype(np.float32)),
        rown=np.ascontiguousarray(
            rout_full[2048 * core:2048 * (core + 1)].astype(np.float32)),
        x_own=np.ascontiguousarray(
            x_flat[2048 * core:2048 * (core + 1)].astype(np.float32)),
        h_own=np.ascontiguousarray(h_full[2048 * core:2048 * (core + 1)]),
        mpre=mpre,
        triu=np.triu(np.ones((128, 128), np.float32), 1).astype(BF),
        tof3=np.ascontiguousarray(tof3),
    )



def kernel(L0_attn_x, L0_attn_W_in, L0_attn_a_param, L0_ffn_w_cls,
           L0_ffn_bias, L0_ffn_w_gate_up, L0_ffn_w_down):
    import numpy as np
    x = np.ascontiguousarray(np.asarray(L0_attn_x, dtype=np.float32))
    W_in = np.asarray(L0_attn_W_in, dtype=np.float32)
    a_param = np.asarray(L0_attn_a_param, dtype=np.float32)
    w_cls = np.asarray(L0_ffn_w_cls, dtype=np.float32)
    bias = np.asarray(L0_ffn_bias, dtype=np.float32)
    w_gu = np.asarray(L0_ffn_w_gate_up, dtype=np.float32)
    w_dn = np.asarray(L0_ffn_w_down, dtype=np.float32)
    from concourse.bass_utils import run_bass_kernel_spmd
    cores = list(range(8))

    nc1 = make_k1()
    res1 = run_bass_kernel_spmd(
        nc1, [k1_inputs(x, W_in, w_cls, bias, a_param, c) for c in cores],
        core_ids=cores)
    wrf_full = np.concatenate(
        [np.asarray(res1.results[2 * b + H]["wr_out"])[2048 * H:2048 * (H + 1)]
         for b in range(4) for H in range(2)], axis=0)
    h_full = np.concatenate(
        [np.concatenate([np.asarray(res1.results[2 * b]["h_out"]).T,
                         np.asarray(res1.results[2 * b + 1]["h_out"]).T], axis=1)
         for b in range(4)], axis=0)

    x_flat = x.reshape(16384, 2048)
    nc2 = make_k2()
    res2 = run_bass_kernel_spmd(
        nc2, [k2_inputs(x_flat, wrf_full, w_gu, w_dn, c) for c in cores],
        core_ids=cores)
    ytab_full = np.concatenate(
        [np.asarray(res2.results[c]["ytab"]) for c in cores], axis=0)

    nc3 = make_k3()
    res3 = run_bass_kernel_spmd(
        nc3, [k3_inputs(ytab_full, wrf_full, x_flat, h_full, c) for c in cores],
        core_ids=cores)
    out = np.concatenate(
        [np.asarray(res3.results[c]["out_own"], np.float32)[0:2048]
         for c in cores], axis=0)
    return np.ascontiguousarray(out.reshape(4, 4096, 2048))
